# version 03
# baseline (speedup 1.0000x reference)
import numpy as np
import ml_dtypes
import concourse.bass as bass
import concourse.mybir as mybir
from concourse.bass_utils import run_bass_kernel_spmd
from contextlib import ExitStack

F32 = mybir.dt.float32
BF16 = mybir.dt.bfloat16
AF = mybir.ActivationFunctionType
ALU = mybir.AluOpType

NCORES = 8
TOK = 2048
D = 2048
ND = 16
NG = 4
GT = 512
NT = 16
NMAT = 5
EPS_LN = 1e-5
EPS_RMS = float(np.finfo(np.float32).eps)


def build(debug_outputs=True):
    nc = bass.Bass("TRN2", target_bir_lowering=False)

    x_sh = nc.dram_tensor("x_sh", [TOK, D], F32, kind="ExternalInput").ap()
    w_sh = {}
    w_sh["wk"] = nc.dram_tensor("wk_sh", [256, D], F32, kind="ExternalInput").ap()
    w_sh["wv"] = nc.dram_tensor("wv_sh", [D, 256], F32, kind="ExternalInput").ap()
    w_sh["wq"] = nc.dram_tensor("wq_sh", [256, D], F32, kind="ExternalInput").ap()
    w_sh["wg"] = nc.dram_tensor("wg_sh", [D, 256], F32, kind="ExternalInput").ap()
    w_sh["wp"] = nc.dram_tensor("wp_sh", [D, 256], F32, kind="ExternalInput").ap()
    kvmask = nc.dram_tensor("kvmask", [1, 8], mybir.dt.int32, kind="ExternalInput").ap()
    wu_sh = nc.dram_tensor("wu_sh", [1024, D], F32, kind="ExternalInput").ap()
    wg2_sh = nc.dram_tensor("wg2_sh", [1024, D], F32, kind="ExternalInput").ap()
    wd_sh = nc.dram_tensor("wd_sh", [D, 1024], F32, kind="ExternalInput").ap()

    y_sh = nc.dram_tensor("y_sh", [TOK, D], F32, kind="ExternalOutput").ap()

    MAT = {"wk": 0, "wv": 1, "wq": 2, "wg": 3, "wp": 4}
    aw_sh = nc.dram_tensor("aw_sh", [NMAT * 2 * 128, D], BF16).ap()
    aw_all = nc.dram_tensor("aw_all", [NCORES * NMAT * 2 * 128, D], BF16, addr_space="Shared").ap()

    def aw_img(m, s):
        row = ((s // 2) * (NMAT * 2) + m * 2 + (s % 2)) * 128
        return aw_all[row:row + 128, :]

    ktj_dram = nc.dram_tensor("ktj_dram", [NT * 128, D], BF16).ap()
    kvacc = nc.dram_tensor("kvacc", [D, D], F32).ap()
    kv_in = nc.dram_tensor("kv_in", [D, D], BF16).ap()
    kv_all = nc.dram_tensor("kv_all", [NCORES * D, D], BF16, addr_space="Shared").ap()
    kvp_dram = nc.dram_tensor("kvp_dram", [D, D], BF16).ap()
    x2_dram = nc.dram_tensor("x2_dram", [TOK, D], F32).ap()
    I8 = mybir.dt.int8
    ffn_sh = nc.dram_tensor("ffn_sh", [24 * 128, D], I8).ap()
    ffn_all = nc.dram_tensor("ffn_all", [NCORES * 24 * 128, D], I8, addr_space="Shared").ap()
    ar_in = nc.dram_tensor("ar_in", [1, 4], F32).ap()
    ar_out = nc.dram_tensor("ar_out", [1, 4], F32).ap()

    def ffn_img(m, blk):
        row = ((blk // 8) * 24 + m * 8 + (blk % 8)) * 128
        return ffn_all[row:row + 128, :]

    with ExitStack() as stk:
        e = stk.enter_context

        def sb(name, shape, dt):
            return e(nc.sbuf_tensor(name, shape, dt))

        def psum(name, shape, dt):
            return e(nc.psum_tensor(name, shape, dt))

        x_sb = sb("x_sb", [128, 2 * D], F32)
        xn_sb = sb("xn_sb", [128, 2 * D], BF16)
        xnT_g = sb("xnT_g", [128, ND * GT], BF16)
        v_sb = sb("v_sb", [128, ND * D], BF16)
        ktj_sb = sb("ktj_sb", [128, 4 * D], BF16)
        qT_g = ktj_sb
        stats = sb("stats", [128, NT * 8], F32)
        bnbuf = sb("bnbuf", [128, 24], F32)
        maggr = sb("maggr", [128, 2 * NT], F32)
        cs_all = sb("cs_all", [128, NT], F32)
        m_all = sb("m_all", [128, NT], F32)
        wo_buf = sb("wo_buf", [128, 2 * D], BF16)
        wt_buf = sb("wt_buf", [128, 2 * 512], BF16)

        idn = sb("idn", [128, 128], BF16)
        ktmp = sb("ktmp", [128, 2 * 128], BF16)
        masks = sb("masks", [128, 4 * GT], BF16)
        g_sb = sb("g_sb", [128, 4 * D], BF16)
        sT_buf = xnT_g
        ogT_g = sb("ogT_g", [128, ND * GT], BF16)
        og_sb = sb("og_sb", [128, 1024], BF16)
        kvp_buf = sb("kvp_buf", [128, 2 * 1024], BF16)
        ktjs_buf = sb("ktjs_buf", [128, 2 * D], BF16)
        wp_buf = sb("wp_buf", [128, 2 * 1024], BF16)
        att_sb = sb("att_sb", [128, 2 * 1024], F32)
        acc_g = att_sb
        kvt_buf = ogT_g
        nat_sb = wp_buf
        img_sb_t = ktjs_buf
        mk8i = sb("mk8i", [1, 8], mybir.dt.int32)
        tern_sb = sb("tern_sb", [128, D], BF16)
        fpc_sb = sb("fpc_sb", [128, D], F32)
        fimg_sb = sb("fimg_sb", [128, D], mybir.dt.int8)
        absum = sb("absum", [128, 4], F32)
        svals = sb("svals", [1, 16], F32)
        scbc = sb("scbc", [128, 16], F32)
        ones_col = sb("ones_col", [128, 1], F32)

        pb = [psum(f"pb{i}", [128, 512], F32) for i in range(8)]
        TPA, TPB = 6, 7
        PKA, PKB = 4, 5

        class Sem:
            def __init__(self, name):
                self.h = e(nc.semaphore(name))
                self.n = 0

            def inc(self, inst, k=1):
                inst.then_inc(self.h, k)
                self.n += k
                return self.n

        s_ini = Sem("s_ini")
        s_pl = Sem("s_pl")
        s_mk = Sem("s_mk")
        s_po = Sem("s_po")
        s_cc = Sem("s_cc")
        s_x = [Sem("s_xA"), Sem("s_xB")]
        tile_ctr = [0]
        sv = Sem("sv")
        sc_ = Sem("sc")
        s_xn = Sem("s_xn")
        s_xt = [Sem("s_xtA"), Sem("s_xtB")]
        s_kt2 = [Sem("s_kt2A"), Sem("s_kt2B")]
        s_kmm = Sem("s_kmm")
        s_ogt2 = Sem("s_ogt2")
        s_tp = Sem("s_tp")
        s_tpe = Sem("s_tpe")
        s_te = Sem("s_te")
        s_wo = [Sem("s_woA"), Sem("s_woB")]
        s_kp = Sem("s_kp")
        s_kev = Sem("s_kev")
        s_wt = [Sem("s_wtA"), Sem("s_wtB")]
        s_vd = Sem("s_vd")
        s_vev = Sem("s_vev")
        s_kvp = Sem("s_kvp")
        s_kve = Sem("s_kve")
        s_kvo = Sem("s_kvo")
        s_ks = Sem("s_ks")
        s_sum = Sem("s_sum")
        s_pu = Sem("s_pu")
        s_sm = [Sem("s_smA"), Sem("s_smB")]
        s_kvw = Sem("s_kvw")
        s_qe = Sem("s_qe")
        s_ge = Sem("s_ge")
        s_kj = [Sem("s_kjA"), Sem("s_kjB")]
        s_kjd = Sem("s_kjd")
        s_se = Sem("s_se")
        s_kvl = [Sem("s_kvlA"), Sem("s_kvlB")]
        s_kvd = Sem("s_kvd")
        s_og = Sem("s_og")
        s_ogt = Sem("s_ogt")
        s_oge = Sem("s_oge")
        s_wp = [Sem("s_wpA"), Sem("s_wpB")]
        s_wpd = Sem("s_wpd")
        s_aev = Sem("s_aev")
        s_acc = Sem("s_acc")
        s_x2i = Sem("s_x2i")
        s_fin = Sem("s_fin")
        s_scb = Sem("s_scb")
        s_ar2 = Sem("s_ar2")
        s_fb = Sem("s_fb")
        s_fl = Sem("s_fl")
        s_fl2 = Sem("s_fl2")
        s_fa = Sem("s_fa")
        s_ar = Sem("s_ar")
        s_tn = Sem("s_tn")
        s_fo = Sem("s_fo")
        s_fu = Sem("s_fu")
        s_fu2 = Sem("s_fu2")
        s_fg = Sem("s_fg")
        s_fg2 = Sem("s_fg2")
        s_fd = Sem("s_fd")
        s_fd2 = Sem("s_fd2")
        s_fud = Sem("s_fud")
        s_t1 = Sem("s_t1")
        s_hid = Sem("s_hid")
        s_dd = Sem("s_dd")
        s_fev = Sem("s_fev")
        s_yo = Sem("s_yo")
        s_x2l = [Sem(f"s_x2l{i}") for i in range(4)]

        block = e(nc.Block())

        def sec(eng, f):
            getattr(block, eng)(f)

        tp_units = [0]
        tile_marks = []
        xt_marks = []

        def bank_guard(t):
            for s in (s_kev, s_se, s_vev, s_kve, s_oge, s_aev, s_tpe, s_og,
                      s_fev, s_hid, s_t1, s_dd, s_fud, s_scb):
                if s.n > 0:
                    t.wait_ge(s.h, s.n)

        def emit_transpose_unit(t, srcs, extra_wait=None):
            pg = tp_units[0]
            if pg >= 2:
                t.wait_ge(s_tpe.h, pg - 1)
            if extra_wait is not None:
                extra_wait(t)
            slot = pb[TPA + (pg % 2)][:, :].bitcast(BF16)
            last = None
            for idx, in_ap in enumerate(srcs):
                last = t.transpose(slot[:, idx * 128:(idx + 1) * 128], in_ap, idn[:, :])
            s_tp.inc(last)
            tp_units[0] += 1
            return pg

        def emit_evac_unit(v, pg, dsts):
            v.wait_ge(s_tp.h, pg + 1)
            slot = pb[TPA + (pg % 2)][:, :].bitcast(BF16)
            last = None
            for idx, out_ap in enumerate(dsts):
                last = v.tensor_copy(out_ap, slot[:, idx * 128:(idx + 1) * 128])
            s_tpe.inc(last)
            return last

        def _init(g):
            idn_f = x_sb[:, 0:128]
            g.iota(idn_f, pattern=[[-1, 128]], channel_multiplier=1,
                   allow_small_or_imprecise_dtypes=True)
            g.drain()
            g.tensor_single_scalar(idn[:, :], idn_f, 0.0, ALU.is_equal)
            g.drain()
            for r in range(4):
                g.iota(x_sb[:, r * GT:(r + 1) * GT], pattern=[[1, GT]],
                       channel_multiplier=-1, base=-128 * r,
                       allow_small_or_imprecise_dtypes=True)
            g.drain()
            last = None
            for r in range(4):
                last = g.tensor_single_scalar(masks[:, r * GT:(r + 1) * GT],
                                              x_sb[:, r * GT:(r + 1) * GT], 0.0, ALU.is_ge)
            g.memset(absum[:, :], 0.0)
            g.memset(ones_col[:, :], 1.0)
            s_ini.inc(last)

        sec("gpsimd", _init)

        def sy_x2i(s):
            s_x2i.inc(s.dma_start(out=x2_dram, in_=x_sh), 16)
        sec("sync", sy_x2i)

        def g_mk(g):
            s_mk.inc(g.dma_start(out=mk8i[0:1, :], in_=kvmask), 16)
        sec("gpsimd", g_mk)
        mk_pl_t = s_mk.n

        def emit_prep():
            for bi, (mname, sl) in enumerate(
                    [(m, s) for m in ["wk", "wv", "wq", "wg", "wp"] for s in range(2)]):
                m = MAT[mname]
                type_o = mname in ("wk", "wq")

                def g_load(g, mname=mname, sl=sl, bi=bi, type_o=type_o,
                           tp_before=tp_units[0]):
                    if bi > 0:
                        g.wait_ge(s_tp.h, tp_before)
                    if type_o:
                        s_pl.inc(g.dma_start(out=nat_sb[:, :],
                                             in_=w_sh[mname][sl * 128:(sl + 1) * 128, :]), 16)
                    else:
                        for r in range(16):
                            s_pl.inc(g.dma_start(
                                out=nat_sb[:, r * 128:(r + 1) * 128],
                                in_=w_sh[mname][r * 128:(r + 1) * 128, sl * 128:(sl + 1) * 128]), 16)
                sec("gpsimd", g_load)
                pl_target = s_pl.n

                pgs = []

                def t_tr(t, pl_target=pl_target, pgs=pgs):
                    def w0(tt):
                        tt.wait_ge(s_ini.h, 1)
                        tt.wait_ge(s_pl.h, pl_target)
                    for pr in range(8):
                        pgs.append(emit_transpose_unit(
                            t,
                            [nat_sb[:, (2 * pr) * 128:(2 * pr + 1) * 128],
                             nat_sb[:, (2 * pr + 1) * 128:(2 * pr + 2) * 128]],
                            extra_wait=w0 if pr == 0 else None))
                sec("tensor", t_tr)

                def v_ev(v, pgs=pgs, po_before=s_po.n, bi=bi):
                    if bi > 0:
                        v.wait_ge(s_po.h, po_before)
                    for pr in range(8):
                        emit_evac_unit(v, pgs[pr], [
                            img_sb_t[:, (2 * pr) * 128:(2 * pr + 1) * 128],
                            img_sb_t[:, (2 * pr + 1) * 128:(2 * pr + 2) * 128]])
                sec("vector", v_ev)

                def g_out(g, m=m, sl=sl, pe_target=s_tpe.n):
                    g.wait_ge(s_tpe.h, pe_target)
                    s_po.inc(g.dma_start(
                        out=aw_sh[(m * 2 + sl) * 128:(m * 2 + sl + 1) * 128, :],
                        in_=img_sb_t[:, 0:D]), 16)
                sec("gpsimd", g_out)

            def g_ag(g, po_target=s_po.n):
                g.wait_ge(s_po.h, po_target)
                s_cc.inc(g.collective_compute(
                    "AllGather", ALU.bypass, replica_groups=[list(range(NCORES))],
                    ins=[aw_sh], outs=[aw_all]))
            sec("gpsimd", g_ag)

        emit_prep()

        slab_f = g_sb[:, :].bitcast(F32)

        def slab_load(g, k, slot):
            if k < 8:
                g.dma_start(out=slab_f[:, slot * D:(slot + 1) * D],
                            in_=wu_sh[k * 128:(k + 1) * 128, :]).then_inc(
                                (s_fl if slot == 0 else s_fl2).h, 16)
            elif k < 16:
                r = k - 8
                g.dma_start(out=slab_f[:, slot * D:(slot + 1) * D],
                            in_=wg2_sh[r * 128:(r + 1) * 128, :]).then_inc(
                                (s_fl if slot == 0 else s_fl2).h, 16)
            else:
                s = k - 16
                for r in range(16):
                    g.dma_start(out=slab_f[:, slot * D + r * 128: slot * D + (r + 1) * 128],
                                in_=wd_sh[r * 128:(r + 1) * 128, s * 128:(s + 1) * 128]
                                ).then_inc((s_fl if slot == 0 else s_fl2).h, 16)
            (s_fl if slot == 0 else s_fl2).n += 16 * (1 if k < 16 else 16)

        def g_fpA_chunk(g, ks):
            for k in ks:
                if k >= 2:
                    g.wait_ge(s_fa.h, k - 1)
                slab_load(g, k, k % 2)
        fpA_marks = []
        _fl = [s_fl.n, s_fl2.n]
        _t0, _t1 = 0, 0
        for k in range(24):
            n = 16 * (1 if k < 16 else 16)
            if k % 2 == 0:
                _t0 += n
                fpA_marks.append((0, _t0))
            else:
                _t1 += n
                fpA_marks.append((1, _t1))

        def v_fpA_chunk(v, ks):
            if ks[0] == 0:
                v.wait_ge(s_ini.h, 1)
            for k in ks:
                slot, target = fpA_marks[k]
                v.wait_ge((s_fl if slot == 0 else s_fl2).h, target)
                m = 0 if k < 8 else (1 if k < 16 else 2)
                v.tensor_reduce(absum[:, 3:4], slab_f[:, slot * D:(slot + 1) * D],
                                mybir.AxisListType.X, ALU.add, apply_absolute_value=True)
                v.drain()
                s_fa.inc(v.tensor_add(absum[:, m:m + 1], absum[:, m:m + 1], absum[:, 3:4]))
                v.drain()

        def emit_xn_phase(g, with_stats):
            g4 = g * 4
            for i in range(4):
                ti = g4 + i
                n_tile = tile_ctr[0]
                tile_ctr[0] += 1
                xslot = n_tile % 2
                sxs = s_x[xslot]

                def sy_x(s, ti=ti, xslot=xslot, sxs=sxs, n_tile=n_tile):
                    if n_tile == 0:
                        s.wait_ge(s_ini.h, 1)
                    if s_xn.n >= 2:
                        s.wait_ge(s_xn.h, s_xn.n - 1)
                    sxs.inc(s.dma_start(out=x_sb[:, xslot * D:(xslot + 1) * D],
                                        in_=x_sh[ti * 128:(ti + 1) * 128, :]), 16)
                sec("sync", sy_x)
                x_sem = sxs
                x_target = sxs.n

                base = ti * 8
                col = lambda k, base=base: stats[:, base + k:base + k + 1]
                w1_c, sq1_c, r1_c, t_c, t2_c, w2_c, sq2_c, r2_c = [col(k) for k in range(8)]
                m_c = maggr[:, 2 * ti:2 * ti + 1]
                v_c = maggr[:, 2 * ti + 1:2 * ti + 2]

                if with_stats:
                    def v_stats(v, ti=ti, xslot=xslot, x_target=x_target,
                                w1_c=w1_c, sq1_c=sq1_c, r1_c=r1_c, t_c=t_c,
                                t2_c=t2_c, w2_c=w2_c, sq2_c=sq2_c, r2_c=r2_c,
                                m_c=m_c, v_c=v_c, n_tile=n_tile, x_sem=x_sem):
                        v.wait_ge(x_sem.h, x_target)
                        for cq in range(4):
                            v.bn_stats(bnbuf[:, cq * 6:(cq + 1) * 6],
                                       x_sb[:, xslot * D + cq * 512: xslot * D + (cq + 1) * 512])
                        v.drain()
                        v.bn_aggr(maggr[:, 2 * ti:2 * ti + 2], bnbuf[:, 0:24])
                        v.drain()
                        sv.inc(v.tensor_scalar_add(w1_c, v_c, EPS_LN))
                        v.wait_ge(sc_.h, sc_.n + 1)
                        v.reciprocal(r1_c, sq1_c)
                        v.drain()
                        v.tensor_mul(t_c, v_c, r1_c)
                        v.drain()
                        v.tensor_mul(t2_c, t_c, r1_c)
                        v.drain()
                        sv.inc(v.tensor_scalar_add(w2_c, t2_c, EPS_RMS))
                        v.wait_ge(sc_.h, sc_.n + 2)
                        v.reciprocal(r2_c, sq2_c)
                        v.drain()
                        v.tensor_mul(cs_all[:, ti:ti + 1], r1_c, r2_c)
                        v.tensor_copy(m_all[:, ti:ti + 1], maggr[:, 2 * ti:2 * ti + 1])
                        v.drain()
                        if n_tile >= 2:
                            pr_, vv_ = xt_marks[n_tile - 2]
                            v.wait_ge(s_xt[pr_].h, vv_)
                        s_xn.inc(v.tensor_scalar(
                            xn_sb[:, (ti % 2) * D:(ti % 2 + 1) * D],
                            x_sb[:, xslot * D:(xslot + 1) * D],
                            m_c, cs_all[:, ti:ti + 1], ALU.subtract, ALU.mult))
                    sec("vector", v_stats)

                    def sc_chain(s, w1_c=w1_c, sq1_c=sq1_c, w2_c=w2_c, sq2_c=sq2_c,
                                 sv_base=sv.n):
                        s.wait_ge(sv.h, sv_base - 1)
                        sc_.inc(s.activation(sq1_c, w1_c, AF.Sqrt))
                        s.wait_ge(sv.h, sv_base)
                        sc_.inc(s.activation(sq2_c, w2_c, AF.Sqrt))
                    sec("scalar", sc_chain)
                else:
                    def v_norm(v, ti=ti, xslot=xslot, x_target=x_target, n_tile=n_tile,
                               x_sem=x_sem):
                        v.wait_ge(x_sem.h, x_target)
                        if n_tile >= 2:
                            pr_, vv_ = xt_marks[n_tile - 2]
                            v.wait_ge(s_xt[pr_].h, vv_)
                        s_xn.inc(v.tensor_scalar(
                            xn_sb[:, (ti % 2) * D:(ti % 2 + 1) * D],
                            x_sb[:, xslot * D:(xslot + 1) * D],
                            m_all[:, ti:ti + 1], cs_all[:, ti:ti + 1],
                            ALU.subtract, ALU.mult))
                    sec("vector", v_norm)

                xn_target = s_xn.n

                def sc_xtr(s, ti=ti, i=i, xn_target=xn_target, n_tile=n_tile):
                    s.wait_ge(s_xn.h, xn_target)
                    for dt_ in range(ND):
                        s_xt[n_tile % 2].inc(s.dma_start_transpose(
                            xnT_g[:, dt_ * GT + i * 128: dt_ * GT + (i + 1) * 128],
                            xn_sb[:, (ti % 2) * D + dt_ * 128: (ti % 2) * D + (dt_ + 1) * 128]), 16)
                sec("scalar", sc_xtr)
                xt_marks.append((n_tile % 2, s_xt[n_tile % 2].n))

        def emit_typeO_proj(g, mname, act, evac_fn):
            m = MAT[mname]
            for o in range(16):
                u_kp = s_kp.n

                wos = s_wo[u_kp % 2]

                def g_w(gp, o=o, m=m, u_kp=u_kp, wos=wos):
                    if u_kp == 0:
                        gp.wait_ge(s_cc.h, 1)
                    if u_kp >= 2:
                        gp.wait_ge(s_kp.h, u_kp - 1)
                    wos.inc(gp.dma_start(
                        out=wo_buf[:, (u_kp % 2) * D:(u_kp % 2 + 1) * D],
                        in_=aw_img(m, o)), 16)
                sec("sync", g_w)

                def t_kp(t, o=o, u=u_kp, wos=wos, wo_target=wos.n,
                         first=(o == 0), xtA=s_xt[0].n, xtB=s_xt[1].n):
                    if first:
                        bank_guard(t)
                        t.wait_ge(s_xt[0].h, xtA)
                        t.wait_ge(s_xt[1].h, xtB)
                    t.wait_ge(wos.h, wo_target)
                    if u >= 2:
                        t.wait_ge(s_kev.h, u - 1)
                    pps = pb[PKA + (u % 2)]
                    mm = None
                    for d in range(ND):
                        mm = t.matmul(pps[:, :],
                                      wo_buf[:, (u % 2) * D + d * 128:(u % 2) * D + (d + 1) * 128],
                                      xnT_g[:, d * GT:(d + 1) * GT],
                                      start=(d == 0), stop=(d == ND - 1))
                    s_kp.inc(mm)
                sec("tensor", t_kp)

                def s_ev(s, o=o, u=u_kp):
                    s.wait_ge(s_kp.h, u + 1)
                    evac_fn(s, o, u, pb[PKA + (u % 2)])
                sec("scalar", s_ev)

        def emit_typeT_proj(g, mname, act, out_fn):
            m = MAT[mname]
            for eq in range(4):
                u_vp = (s_vd.n // 16)

                for d in range(ND):
                    w_wt = (s_wt[0].n + s_wt[1].n) // 16
                    wts = s_wt[w_wt % 2]

                    def g_w(gp, m=m, eq=eq, d=d, w=w_wt, wts=wts):
                        if w >= 2:
                            gp.wait_ge(s_vd.h, w - 1)
                        wts.inc(gp.dma_start(
                            out=wt_buf[:, (w % 2) * 512:(w % 2 + 1) * 512],
                            in_=aw_img(m, d)[:, eq * 512:(eq + 1) * 512]), 16)
                    sec("sync", g_w)

                    def t_vp(t, eq=eq, d=d, w=w_wt, u_vp=u_vp, wts=wts, wt_target=wts.n,
                             vev_before=s_vev.n, first=(eq == 0 and d == 0),
                             xtA=s_xt[0].n, xtB=s_xt[1].n):
                        if first:
                            bank_guard(t)
                            t.wait_ge(s_xt[0].h, xtA)
                            t.wait_ge(s_xt[1].h, xtB)
                        t.wait_ge(wts.h, wt_target)
                        if d == 0 and u_vp >= 1:
                            t.wait_ge(s_vev.h, vev_before)
                        mm = None
                        for i in range(4):
                            mm = t.matmul(pb[i][:, :],
                                          xnT_g[:, d * GT + i * 128: d * GT + (i + 1) * 128],
                                          wt_buf[:, (w % 2) * 512:(w % 2 + 1) * 512],
                                          start=(d == 0), stop=(d == ND - 1))
                        s_vd.inc(mm)
                    sec("tensor", t_vp)

                def s_ev(s, eq=eq, act=act, out_fn=out_fn, vd_t=s_vd.n):
                    s.wait_ge(s_vd.h, vd_t)
                    for i in range(4):
                        s_vev.inc(s.activation(out_fn(i, eq), pb[i][:, :], act))
                sec("scalar", s_ev)

        for g in range(NG):
            sec("gpsimd", lambda gp, ks=list(range(g * 6, g * 6 + 6)): g_fpA_chunk(gp, ks))
            emit_xn_phase(g, with_stats=True)
            sec("vector", lambda v, ks=list(range(g * 6, g * 6 + 6)): v_fpA_chunk(v, ks))

            ks_before = s_ks.n

            def k_evac(s, o, u, pps, g=g, ks_before=ks_before):
                if o == 0 and ks_before > 0:
                    s.wait_ge(s_ks.h, ks_before)
                last = None
                for i in range(4):
                    last = s.activation(ktj_sb[:, i * D + o * 128: i * D + (o + 1) * 128],
                                        pps[:, i * 128:(i + 1) * 128], AF.Silu)
                s_kev.inc(last)
            emit_typeO_proj(g, "wk", AF.Silu, k_evac)

            emit_typeT_proj(g, "wv", AF.Silu,
                            lambda i, eq, g=g: v_sb[:, (g * 4 + i) * D + eq * 512:
                                                    (g * 4 + i) * D + (eq + 1) * 512])

            kev_t = s_kev.n
            vev_t = s_vev.n
            for o in range(16):
                for j in range(4):
                    kj = g * 64 + o * 4 + j
                    kts = s_kt2[kj % 2]

                    def sc_ktr(s, g=g, o=o, j=j, kj=kj, kts=kts, kmm_t=s_kmm.n,
                               kev_t=kev_t):
                        if kj >= 2:
                            s.wait_ge(s_kmm.h, kmm_t - 1)
                        if o == 0 and j == 0:
                            s.wait_ge(s_kev.h, kev_t)
                        kts.inc(s.dma_start_transpose(
                            ktmp[:, (kj % 2) * 128:(kj % 2 + 1) * 128],
                            ktj_sb[:, j * D + o * 128: j * D + (o + 1) * 128]), 16)
                    sec("scalar", sc_ktr)

                    def t_kv1(t, g=g, o=o, j=j, kj=kj, kts=kts, kt_target=kts.n,
                              vev_t=vev_t):
                        if o == 0 and j == 0:
                            bank_guard(t)
                            t.wait_ge(s_vev.h, vev_t)
                        if o >= 1 and j == 0:
                            t.wait_ge(s_kve.h, g * 16 + o)
                        t.wait_ge(kts.h, kt_target)
                        mm = None
                        for er in range(4):
                            mm = t.matmul(pb[er][:, :],
                                          ktmp[:, (kj % 2) * 128:(kj % 2 + 1) * 128],
                                          v_sb[:, (g * 4 + j) * D + er * 512:
                                               (g * 4 + j) * D + (er + 1) * 512],
                                          start=(j == 0), stop=(j == 3))
                        s_kmm.inc(mm)
                    sec("tensor", t_kv1)

                def v_kve(v, g=g, o=o, kvo_before=s_kvo.n, kmm_t=s_kmm.n):
                    v.wait_ge(s_kmm.h, kmm_t)
                    if kvo_before >= 16:
                        v.wait_ge(s_kvo.h, kvo_before)
                    last = None
                    for er in range(4):
                        last = v.tensor_copy(fpc_sb[:, er * 512:(er + 1) * 512], pb[er][:, :])
                    s_kve.inc(last)
                sec("vector", v_kve)

                def sy_kv(s, g=g, o=o):
                    s.wait_ge(s_kve.h, g * 16 + o + 1)
                    if g == 0:
                        s_kvo.inc(s.dma_start(out=kvacc[o * 128:(o + 1) * 128, :],
                                              in_=fpc_sb[:, :]), 16)
                    else:
                        s_kvo.inc(s.dma_start(out=kvacc[o * 128:(o + 1) * 128, :],
                                              in_=fpc_sb[:, :], accum_op=ALU.add), 16)
                    s.wait_ge(s_kvo.h, s_kvo.n)
                sec("gpsimd", sy_kv)

            def sy_ks(s, g=g, ktA_t=s_kt2[0].n, ktB_t=s_kt2[1].n):
                s.wait_ge(s_kt2[0].h, ktA_t)
                s.wait_ge(s_kt2[1].h, ktB_t)
                for j in range(4):
                    s_ks.inc(s.dma_start(
                        out=ktj_dram[(g * 4 + j) * 128:(g * 4 + j + 1) * 128, :],
                        in_=ktj_sb[:, j * D:(j + 1) * D]), 16)
            sec("sync", sy_ks)


        def t_fpB1(t, fa_t=s_fa.n, ini_t=1):
            bank_guard(t)
            t.wait_ge(s_ini.h, ini_t)
            t.wait_ge(s_fa.h, fa_t)
            s_fb.inc(t.matmul(pb[PKA][0:4, 0:1], absum[:, 0:4], ones_col[:, :],
                              start=True, stop=True))
        sec("tensor", t_fpB1)
        fpB1_kp = s_fb.n

        def v_fpB1(v, kp_t=fpB1_kp):
            v.wait_ge(s_fb.h, kp_t)
            s_fa.inc(v.tensor_copy(absum[0:4, 3:4], pb[PKA][0:4, 0:1]))
        sec("vector", v_fpB1)
        fpB1_fa = s_fa.n

        onesf = ktmp[:, :].bitcast(F32)[0:1, 0:128]

        def g_fpB(g, fa_t=fpB1_fa, kvo_t=s_kvo.n):
            g.wait_ge(s_fa.h, fa_t)
            g.wait_ge(s_kvo.h, kvo_t)
            g.memset(onesf, 1.0)
            g.drain()
            g.dma_start(out=ar_in, in_=absum[0:4, 3:4]).then_inc(s_ar.h, 16)
            s_ar.n += 16
            g.wait_ge(s_ar.h, s_ar.n)
            s_cc.inc(g.collective_compute(
                "AllReduce", ALU.add, replica_groups=[list(range(NCORES))],
                ins=[ar_in], outs=[ar_out]))
            g.wait_ge(s_cc.h, 2)
            g.dma_start(out=svals[0:1, 0:4], in_=ar_out).then_inc(s_ar.h, 16)
            s_ar.n += 16
        sec("gpsimd", g_fpB)
        ar_t = s_ar.n

        def v_fpB2(v, ar_t=ar_t):
            v.wait_ge(s_ar.h, ar_t)
            v.tensor_scalar(svals[0:1, 4:7], svals[0:1, 0:3], 2.0 ** -24, 1e-8,
                            ALU.mult, ALU.add)
            v.drain()
            v.tensor_mul(svals[0:1, 7:8], svals[0:1, 5:6], svals[0:1, 6:7])
            v.tensor_scalar_mul(svals[0:1, 8:11], svals[0:1, 4:7], 0.5)
            v.drain()
            v.tensor_scalar_mul(svals[0:1, 11:14], svals[0:1, 8:11], -1.0)
            v.tensor_scalar_mul(svals[0:1, 14:15], svals[0:1, 4:5], -1.0)
            v.tensor_scalar_mul(svals[0:1, 15:16], svals[0:1, 7:8], -1.0)
            s_ar2.inc(v.drain())
        sec("vector", v_fpB2)

        def t_fpB2(t, ar2_t=s_ar2.n):
            bank_guard(t)
            t.wait_ge(s_ar2.h, ar2_t)
            s_fb.inc(t.matmul(pb[PKB][:, 0:16], onesf[0:1, :], svals[0:1, 0:16],
                              start=True, stop=True))
        sec("tensor", t_fpB2)
        fpB2_kp = s_fb.n

        def v_fpB3(v, kp_t=fpB2_kp):
            v.wait_ge(s_fb.h, kp_t)
            s_scb.inc(v.tensor_copy(scbc[:, :], pb[PKB][:, 0:16]))
        sec("vector", v_fpB3)

        def g_kvsum(g, kvo_t=s_kvo.n, mk_pl_t=mk_pl_t):
            kvt_f = ogT_g[:, :].bitcast(F32)
            g.wait_ge(s_kvo.h, kvo_t)
            g.wait_ge(s_mk.h, mk_pl_t)
            g.dma_start(out=kv_in, in_=kvacc).then_inc(s_sum.h, 16)
            s_sum.n += 16
            g.wait_ge(s_sum.h, s_sum.n)
            s_cc.inc(g.collective_compute(
                "AllGather", ALU.bypass, replica_groups=[list(range(NCORES))],
                ins=[kv_in], outs=[kv_all]))
            g.wait_ge(s_cc.h, 3)
            with g.register("mreg") as mreg:
                for ds in range(16):
                    g.memset(acc_g[:, :], 0.0)
                    g.drain()
                    base_units = s_pu.n

                    def load(jj):
                        sm = s_sm[jj % 2]
                        if jj >= 2:
                            g.wait_ge(s_pu.h, base_units + jj - 1)
                        g.dma_start(out=kvt_f[:, (jj % 2) * D:(jj % 2 + 1) * D],
                                    in_=kv_all[(jj * 16 + ds) * 128:
                                               (jj * 16 + ds + 1) * 128, :]
                                    ).then_inc(sm.h, 16)
                        sm.n += 16

                    load(0)
                    for jj in range(8):
                        if jj + 1 < 8:
                            load(jj + 1)
                        sm = s_sm[jj % 2]
                        g.wait_ge(sm.h, sm.n)
                        g.reg_load(mreg, mk8i[0:1, jj:jj + 1])
                        with g.If_eq(mreg, 1):
                            g.tensor_add(acc_g[:, :], acc_g[:, :],
                                         kvt_f[:, (jj % 2) * D:(jj % 2 + 1) * D]
                                         ).then_inc(s_pu.h, 1)
                        with g.Else():
                            g.sem_inc(s_pu.h, 1)
                        s_pu.n += 1
                    g.drain()
                    g.tensor_copy(kvt_f[:, 0:D].bitcast(BF16)[:, 0:D], acc_g[:, :])
                    g.drain()
                    g.dma_start(out=kvp_dram[ds * 128:(ds + 1) * 128, :],
                                in_=kvt_f[:, 0:D].bitcast(BF16)[:, 0:D]).then_inc(s_kvw.h, 16)
                    s_kvw.n += 16
                    g.wait_ge(s_kvw.h, s_kvw.n)
        sec("gpsimd", g_kvsum)

        s_fc = Sem("s_fc")
        s_tn2 = Sem("s_tn2")
        fc_marks = {}

        def g_fpC_load(gp, k):
            if k >= 1:
                gp.wait_ge(s_tn.h, k)
            if k < 8:
                s_fc.inc(gp.dma_start(out=fpc_sb[:, :],
                                      in_=wu_sh[k * 128:(k + 1) * 128, :]), 16)
            elif k < 16:
                r = k - 8
                s_fc.inc(gp.dma_start(out=fpc_sb[:, :],
                                      in_=wg2_sh[r * 128:(r + 1) * 128, :]), 16)
            else:
                s = k - 16
                for r in range(16):
                    s_fc.inc(gp.dma_start(
                        out=fpc_sb[:, r * 128:(r + 1) * 128],
                        in_=wd_sh[r * 128:(r + 1) * 128, s * 128:(s + 1) * 128]), 16)
            fc_marks[k] = s_fc.n

        def v_fpC_tern(v, k, scb_t):
            v.wait_ge(s_fc.h, fc_marks[k])
            v.wait_ge(s_scb.h, scb_t)
            m = 0 if k < 8 else (1 if k < 16 else 2)
            v.tensor_single_scalar(tern_sb[:, :], fpc_sb[:, :],
                                   scbc[:, 8 + m:9 + m], ALU.is_gt)
            v.drain()
            s_tn.inc(v.scalar_tensor_tensor(tern_sb[:, :], fpc_sb[:, :],
                                            scbc[:, 11 + m:12 + m], tern_sb[:, :],
                                            ALU.is_lt, ALU.subtract))

        def emit_fpC_slab(k, scb_t):
            sec("gpsimd", lambda gp, k=k: g_fpC_load(gp, k))
            sec("vector", lambda v, k=k, scb_t=scb_t: v_fpC_tern(v, k, scb_t))
            pgs_f = []

            def t_ftr(t, k=k, pgs_f=pgs_f, tn_t=s_tn.n):
                def w0(tt):
                    tt.wait_ge(s_tn.h, tn_t)
                for pr in range(8):
                    pgs_f.append(emit_transpose_unit(
                        t,
                        [tern_sb[:, (2 * pr) * 128:(2 * pr + 1) * 128],
                         tern_sb[:, (2 * pr + 1) * 128:(2 * pr + 2) * 128]],
                        extra_wait=w0 if pr == 0 else None))
            sec("tensor", t_ftr)

            def v_fev2(v, k=k, pgs_f=pgs_f, fo_t=16 * k):
                if fo_t > 0:
                    v.wait_ge(s_fo.h, fo_t)
                for pr in range(8):
                    emit_evac_unit(v, pgs_f[pr], [
                        fimg_sb[:, (2 * pr) * 128:(2 * pr + 1) * 128],
                        fimg_sb[:, (2 * pr + 1) * 128:(2 * pr + 2) * 128]])
            sec("vector", v_fev2)

            def g_fpC_out(gp, k=k, tpe_t=s_tpe.n):
                gp.wait_ge(s_tpe.h, tpe_t)
                s_fo.inc(gp.dma_start(out=ffn_sh[k * 128:(k + 1) * 128, :],
                                      in_=fimg_sb[:, :]), 16)
            sec("gpsimd", g_fpC_out)

        for g in range(NG):
            emit_xn_phase(g, with_stats=False)

            def q_evac(s, o, u, pps, ks_t=s_ks.n):
                if o == 0:
                    s.wait_ge(s_ks.h, ks_t)
                s_kev.inc(s.activation(qT_g[:, o * GT:(o + 1) * GT], pps[:, :], AF.Silu))
            emit_typeO_proj(g, "wq", AF.Silu, q_evac)

            emit_typeT_proj(g, "wg", AF.Sigmoid,
                            lambda i, eq: g_sb[:, i * D + eq * 512: i * D + (eq + 1) * 512])

            njt = 4 * (g + 1)
            qe_t = s_kev.n
            for jt in range(njt):
                u_kj = (s_kj[0].n + s_kj[1].n) // 16
                kjs = s_kj[u_kj % 2]

                def sy_kj(s, jt=jt, u=u_kj, po_t=s_po.n, kjs=kjs):
                    if u == 0:
                        s.wait_ge(s_po.h, po_t)
                    if u >= 2:
                        s.wait_ge(s_kjd.h, u - 1)
                    kjs.inc(s.dma_start(out=ktjs_buf[:, (u % 2) * D:(u % 2 + 1) * D],
                                        in_=ktj_dram[jt * 128:(jt + 1) * 128, :]), 16)
                sec("sync", sy_kj)

                def t_sT(t, g=g, jt=jt, u=u_kj, kjs=kjs, kj_target=kjs.n, qe_t=qe_t,
                         u_sT=s_se.n, ks_t=s_ks.n):
                    if jt == 0:
                        bank_guard(t)
                        t.wait_ge(s_kev.h, qe_t)
                        t.wait_ge(s_ks.h, ks_t)
                    t.wait_ge(kjs.h, kj_target)
                    if u_sT >= 2:
                        t.wait_ge(s_se.h, u_sT - 1)
                    pps = pb[PKA + (u_sT % 2)]
                    mm = None
                    for ft in range(ND):
                        mm = t.matmul(pps[:, :],
                                      ktjs_buf[:, (u % 2) * D + ft * 128:(u % 2) * D + (ft + 1) * 128],
                                      qT_g[:, ft * GT:(ft + 1) * GT],
                                      start=(ft == 0), stop=(ft == ND - 1))
                    s_kjd.inc(mm)
                sec("tensor", t_sT)

                def v_se(v, g=g, jt=jt, u_sT=s_se.n, kjd_t=s_kjd.n):
                    v.wait_ge(s_kjd.h, kjd_t)
                    pps = pb[PKA + (u_sT % 2)]
                    r = jt - 4 * g
                    if r >= 0:
                        s_se.inc(v.tensor_mul(sT_buf[:, jt * GT:(jt + 1) * GT],
                                              pps[:, :], masks[:, r * GT:(r + 1) * GT]))
                    else:
                        s_se.inc(v.tensor_copy(sT_buf[:, jt * GT:(jt + 1) * GT], pps[:, :]))
                sec("vector", v_se)

            if g <= 1:
                for k in range(g * 12, g * 12 + 12):
                    emit_fpC_slab(k, 1)
            if g == 1:
                def g_fAG(gp, fo_t=s_fo.n):
                    gp.wait_ge(s_fo.h, fo_t)
                    s_cc.inc(gp.collective_compute(
                        "AllGather", ALU.bypass, replica_groups=[list(range(NCORES))],
                        ins=[ffn_sh], outs=[ffn_all]))
                sec("gpsimd", g_fAG)

            se_t = s_se.n
            ge_t = s_vev.n
            for half in range(2):
                kv_sems = []
                for d in range(ND):
                    u_kv = (s_kvl[0].n + s_kvl[1].n) // 16
                    kvs = s_kvl[u_kv % 2]

                    def sy_kvl(s, g=g, half=half, d=d, u=u_kv, kvw_t=s_kvw.n, kvs=kvs):
                        if u == 0:
                            s.wait_ge(s_kvw.h, kvw_t)
                        if u >= 2:
                            s.wait_ge(s_kvd.h, u - 1)
                        kvs.inc(s.dma_start(
                            out=kvp_buf[:, (u % 2) * 1024:(u % 2 + 1) * 1024],
                            in_=kvp_dram[d * 128:(d + 1) * 128, half * 1024:(half + 1) * 1024]), 16)
                    sec("sync", sy_kvl)
                    kv_sems.append((kvs, kvs.n))

                def t_o(t, g=g, half=half, njt=njt, se_t=se_t, ge_t=ge_t,
                        kv_sems=kv_sems):
                    bank_guard(t)
                    t.wait_ge(s_se.h, se_t)
                    t.wait_ge(s_vev.h, ge_t)
                    for i in range(4):
                        first = True
                        for jt in range(4 * g + i + 1):
                            for er in range(2):
                                t.matmul(pb[2 * i + er][:, :],
                                         sT_buf[:, jt * GT + i * 128: jt * GT + (i + 1) * 128],
                                         v_sb[:, jt * D + half * 1024 + er * 512:
                                              jt * D + half * 1024 + (er + 1) * 512],
                                         start=(jt == 0), stop=False)
                    for d in range(ND):
                        t.wait_ge(kv_sems[d][0].h, kv_sems[d][1])
                        mm = None
                        for i in range(4):
                            for er in range(2):
                                mm = t.matmul(pb[2 * i + er][:, :],
                                              qT_g[:, d * GT + i * 128: d * GT + (i + 1) * 128],
                                              kvp_buf[:, (d % 2) * 1024 + er * 512:
                                                      (d % 2) * 1024 + (er + 1) * 512],
                                              start=False, stop=(d == ND - 1))
                        s_kvd.inc(mm)
                sec("tensor", t_o)

                for i in range(4):
                    def v_og(v, g=g, half=half, i=i, kvd_t=s_kvd.n,
                             ogt_before=s_ogt2.n):
                        v.wait_ge(s_kvd.h, kvd_t)
                        if ogt_before > 0:
                            v.wait_ge(s_ogt2.h, ogt_before)
                        last = None
                        for er in range(2):
                            last = v.tensor_mul(
                                og_sb[:, er * 512:(er + 1) * 512],
                                pb[2 * i + er][:, :],
                                g_sb[:, i * D + half * 1024 + er * 512:
                                     i * D + half * 1024 + (er + 1) * 512])
                        s_og.inc(last)
                    sec("vector", v_og)

                    def sc_ogt(s, g=g, half=half, i=i, og_t=s_og.n):
                        s.wait_ge(s_og.h, og_t)
                        for et in range(8):
                            eg = half * 8 + et
                            s_ogt2.inc(s.dma_start_transpose(
                                ogT_g[:, eg * GT + i * 128: eg * GT + (i + 1) * 128],
                                og_sb[:, et * 128:(et + 1) * 128]), 16)
                    sec("scalar", sc_ogt)

            oge_t = s_ogt2.n
            for dmh in range(2):
                wp_sems = []
                for et in range(ND):
                    u_wp = (s_wp[0].n + s_wp[1].n) // 16
                    wps = s_wp[u_wp % 2]

                    def sy_wp(s, g=g, dmh=dmh, et=et, u=u_wp, wps=wps):
                        if u >= 2:
                            s.wait_ge(s_wpd.h, u - 1)
                        wps.inc(s.dma_start(
                            out=wp_buf[:, (u % 2) * 1024:(u % 2 + 1) * 1024],
                            in_=aw_img(MAT["wp"], et)[:, dmh * 1024:(dmh + 1) * 1024]), 16)
                    sec("sync", sy_wp)
                    wp_sems.append((wps, wps.n))

                def t_wp(t, g=g, dmh=dmh, oge_t=oge_t, wp_sems=wp_sems,
                         aev_before=s_aev.n):
                    bank_guard(t)
                    t.wait_ge(s_ogt2.h, oge_t)
                    if aev_before >= 1:
                        t.wait_ge(s_aev.h, aev_before)
                    for et in range(ND):
                        t.wait_ge(wp_sems[et][0].h, wp_sems[et][1])
                        mm = None
                        for i in range(4):
                            for dr in range(2):
                                mm = t.matmul(
                                    pb[2 * i + dr][:, :],
                                    ogT_g[:, et * GT + i * 128: et * GT + (i + 1) * 128],
                                    wp_buf[:, (et % 2) * 1024 + dr * 512:
                                           (et % 2) * 1024 + (dr + 1) * 512],
                                    start=(et == 0), stop=(et == ND - 1))
                        s_wpd.inc(mm)
                sec("tensor", t_wp)

                for i in range(4):
                    def v_aev(v, g=g, dmh=dmh, i=i, wpd_t=s_wpd.n,
                              acc_before=s_acc.n):
                        v.wait_ge(s_wpd.h, wpd_t)
                        if acc_before >= 32:
                            v.wait_ge(s_acc.h, acc_before - 16)
                        last = None
                        for dr in range(2):
                            last = v.tensor_copy(
                                att_sb[:, (i % 2) * 1024 + dr * 512:
                                       (i % 2) * 1024 + (dr + 1) * 512],
                                pb[2 * i + dr][:, :])
                        s_aev.inc(last)
                    sec("vector", v_aev)

                    def g_acc(gp, g=g, dmh=dmh, i=i, aev_t=s_aev.n, x2i_t=s_x2i.n):
                        gp.wait_ge(s_aev.h, aev_t)
                        gp.wait_ge(s_x2i.h, x2i_t)
                        s_acc.inc(gp.dma_start(
                            out=x2_dram[(g * 4 + i) * 128:(g * 4 + i + 1) * 128,
                                        dmh * 1024:(dmh + 1) * 1024],
                            in_=att_sb[:, (i % 2) * 1024:(i % 2 + 1) * 1024],
                            accum_op=ALU.add), 16)
                        gp.wait_ge(s_acc.h, s_acc.n)
                    sec("gpsimd", g_acc)

        x2g_f = [ktj_sb[:, :].bitcast(F32), ogT_g[:, :].bitcast(F32)]

        def x2tile(i):
            return x2g_f[i // 2][:, (i % 2) * D:(i % 2 + 1) * D]

        xn2T = xnT_g
        hidT = v_sb
        t1v = kvp_buf[:, :].bitcast(F32)
        wd_buf2 = wp_buf
        out_sb = att_sb
        s_yo2 = Sem("s_yo2")
        acc_total = s_acc.n

        for g3 in range(NG):
            g4 = g3 * 4
            for i in range(4):
                ti = g4 + i
                n_tile = tile_ctr[0]
                tile_ctr[0] += 1
                sxs = s_x2l[i]

                def sy_x2(s, ti=ti, i=i, g3=g3, sxs=sxs, fev_t=8 * g3):
                    s.wait_ge(s_acc.h, acc_total)
                    if fev_t > 0:
                        s.wait_ge(s_fev.h, fev_t)
                    sxs.inc(s.dma_start(out=x2tile(i),
                                        in_=x2_dram[ti * 128:(ti + 1) * 128, :]), 16)
                sec("sync", sy_x2)
                x2_target = sxs.n

                base = ti * 8
                col = lambda k, base=base: stats[:, base + k:base + k + 1]
                w1_c, sq1_c, r1_c = col(0), col(1), col(2)
                m_c = maggr[:, 2 * ti:2 * ti + 1]
                v_c = maggr[:, 2 * ti + 1:2 * ti + 2]

                def v_stats2(v, ti=ti, i=i, x2_target=x2_target, sxs=sxs,
                             w1_c=w1_c, sq1_c=sq1_c, r1_c=r1_c, m_c=m_c, v_c=v_c,
                             n_tile=n_tile):
                    v.wait_ge(sxs.h, x2_target)
                    for cq in range(4):
                        v.bn_stats(bnbuf[:, cq * 6:(cq + 1) * 6],
                                   x2tile(i)[:, cq * 512:(cq + 1) * 512])
                    v.drain()
                    v.bn_aggr(maggr[:, 2 * ti:2 * ti + 2], bnbuf[:, 0:24])
                    v.drain()
                    sv.inc(v.tensor_scalar_add(w1_c, v_c, EPS_LN))
                    v.wait_ge(sc_.h, sc_.n + 1)
                    v.reciprocal(r1_c, sq1_c)
                    v.drain()
                    v.tensor_copy(cs_all[:, ti:ti + 1], r1_c)
                    v.drain()
                    if n_tile >= 2:
                        pr_, vv_ = xt_marks[n_tile - 2]
                        v.wait_ge(s_xt[pr_].h, vv_)
                    s_xn.inc(v.tensor_scalar(
                        xn_sb[:, (n_tile % 2) * D:(n_tile % 2 + 1) * D],
                        x2tile(i),
                        m_c, cs_all[:, ti:ti + 1], ALU.subtract, ALU.mult))
                sec("vector", v_stats2)

                def sc_chain2(s, w1_c=w1_c, sq1_c=sq1_c, sv_base=sv.n):
                    s.wait_ge(sv.h, sv_base)
                    sc_.inc(s.activation(sq1_c, w1_c, AF.Sqrt))
                sec("scalar", sc_chain2)

                xn_target = s_xn.n

                def sc_xtr2(s, n_tile=n_tile, i=i, xn_target=xn_target):
                    s.wait_ge(s_xn.h, xn_target)
                    for dt_ in range(ND):
                        s_xt[n_tile % 2].inc(s.dma_start_transpose(
                            xn2T[:, dt_ * GT + i * 128: dt_ * GT + (i + 1) * 128],
                            xn_sb[:, (n_tile % 2) * D + dt_ * 128:
                                  (n_tile % 2) * D + (dt_ + 1) * 128]), 16)
                sec("scalar", sc_xtr2)
                xt_marks.append((n_tile % 2, s_xt[n_tile % 2].n))

            xtA_now, xtB_now = s_xt[0].n, s_xt[1].n
            for b in range(64):
                ub = g3 * 64 + b
                slot = ub % 2
                fus = [s_fu, s_fu2][slot]
                fgs = [s_fg, s_fg2][slot]

                def g_ld_ug(gp, b=b, ub=ub, slot=slot, fus=fus, fgs=fgs, cc_t=4):
                    if ub == 0:
                        gp.wait_ge(s_cc.h, cc_t)
                    if ub >= 2:
                        gp.wait_ge(s_hid.h, ub - 1)
                    fus.inc(gp.dma_start(out=wo_buf[:, slot * D:(slot + 1) * D],
                                         in_=ffn_img(0, b)), 16)
                    fgs.inc(gp.dma_start(out=ktjs_buf[:, slot * D:(slot + 1) * D],
                                         in_=ffn_img(1, b)), 16)
                sec("gpsimd", g_ld_ug)
                fu_t, fg_t = fus.n, fgs.n

                def t_ug(t, b=b, ub=ub, slot=slot, fus=fus, fgs=fgs,
                         fu_t=fu_t, fg_t=fg_t, g3=g3, xtA_now=xtA_now, xtB_now=xtB_now,
                         first=(b == 0)):
                    if first:
                        bank_guard(t)
                        t.wait_ge(s_xt[0].h, xtA_now)
                        t.wait_ge(s_xt[1].h, xtB_now)
                    t.wait_ge(fus.h, fu_t)
                    t.wait_ge(fgs.h, fg_t)
                    if ub >= 2:
                        t.wait_ge(s_hid.h, ub - 1)
                    mm = None
                    for d in range(ND):
                        t.matmul(pb[2 * slot][:, :],
                                 wo_buf[:, slot * D + d * 128: slot * D + (d + 1) * 128],
                                 xn2T[:, d * GT:(d + 1) * GT],
                                 start=(d == 0), stop=(d == ND - 1))
                    for d in range(ND):
                        mm = t.matmul(pb[2 * slot + 1][:, :],
                                      ktjs_buf[:, slot * D + d * 128: slot * D + (d + 1) * 128],
                                      xn2T[:, d * GT:(d + 1) * GT],
                                      start=(d == 0), stop=(d == ND - 1))
                    s_fud.inc(mm)
                sec("tensor", t_ug)

                def s_t1f(s, ub=ub, slot=slot, fud_t=s_fud.n, scb_t=1):
                    s.wait_ge(s_fud.h, fud_t)
                    if ub == 0:
                        s.wait_ge(s_scb.h, scb_t)
                    s_t1.inc(s.activation(t1v[:, slot * 512:(slot + 1) * 512],
                                          pb[2 * slot][:, :], AF.Silu,
                                          scale=scbc[:, 14:15]))
                sec("scalar", s_t1f)

                def v_hid(v, b=b, ub=ub, slot=slot, g3=g3, t1_t=s_t1.n):
                    v.wait_ge(s_t1.h, t1_t)
                    s_hid.inc(v.scalar_tensor_tensor(
                        hidT[:, b * 512:(b + 1) * 512],
                        pb[2 * slot + 1][:, :], scbc[:, 15:16],
                        t1v[:, slot * 512:(slot + 1) * 512],
                        ALU.mult, ALU.mult))
                sec("vector", v_hid)

            for dmh in range(2):
                fd_sems = []
                for e8 in range(64):
                    ud = (s_fd.n + s_fd2.n) // 16
                    fds = [s_fd, s_fd2][ud % 2]

                    def g_ld_d(gp, e8=e8, dmh=dmh, ud=ud, fds=fds):
                        if ud >= 2:
                            gp.wait_ge(s_dd.h, ud - 1)
                        fds.inc(gp.dma_start(
                            out=wd_buf2[:, (ud % 2) * 1024:(ud % 2 + 1) * 1024],
                            in_=ffn_img(2, e8)[:, dmh * 1024:(dmh + 1) * 1024]), 16)
                    sec("gpsimd", g_ld_d)
                    fd_sems.append((fds, fds.n, ud % 2))

                def t_dn(t, g3=g3, dmh=dmh, fd_sems=fd_sems, hid_t=s_hid.n):
                    bank_guard(t)
                    t.wait_ge(s_hid.h, hid_t)
                    for e8 in range(64):
                        fds, tgt, slot = fd_sems[e8]
                        t.wait_ge(fds.h, tgt)
                        mm = None
                        for i in range(4):
                            for dr in range(2):
                                mm = t.matmul(
                                    pb[2 * i + dr][:, :],
                                    hidT[:, e8 * 512 + i * 128: e8 * 512 + (i + 1) * 128],
                                    wd_buf2[:, slot * 1024 + dr * 512:
                                            slot * 1024 + (dr + 1) * 512],
                                    start=(e8 == 0), stop=(e8 == 63))
                        s_dd.inc(mm)
                sec("tensor", t_dn)

                for i in range(4):
                    unit = (g3 * 2 + dmh) * 4 + i
                    yos = [s_yo, s_yo2][unit % 2]

                    def v_fevf(v, g3=g3, dmh=dmh, i=i, unit=unit, dd_t=s_dd.n,
                               yos=yos, yo_t=yos.n):
                        v.wait_ge(s_dd.h, dd_t)
                        if yo_t > 0:
                            v.wait_ge(yos.h, yo_t)
                        v.tensor_sub(out_sb[:, (unit % 2) * 1024:(unit % 2) * 1024 + 512],
                                     x2tile(i)[:, dmh * 1024: dmh * 1024 + 512],
                                     pb[2 * i][:, :])
                        s_fev.inc(v.tensor_sub(
                            out_sb[:, (unit % 2) * 1024 + 512:(unit % 2 + 1) * 1024],
                            x2tile(i)[:, dmh * 1024 + 512:(dmh + 1) * 1024],
                            pb[2 * i + 1][:, :]))
                    sec("vector", v_fevf)

                    def sy_y(s, g3=g3, dmh=dmh, i=i, unit=unit, yos=yos,
                             fev_t=s_fev.n):
                        s.wait_ge(s_fev.h, fev_t)
                        yos.inc(s.dma_start(
                            out=y_sh[(g3 * 4 + i) * 128:(g3 * 4 + i + 1) * 128,
                                     dmh * 1024:(dmh + 1) * 1024],
                            in_=out_sb[:, (unit % 2) * 1024:(unit % 2 + 1) * 1024]), 16)
                    sec("sync", sy_y)

        def sy_fin(s, a=s_yo.n, b=s_yo2.n):
            s.wait_ge(s_yo.h, a)
            s.wait_ge(s_yo2.h, b)
        sec("sync", sy_fin)

    return nc




_CACHE = {}


def kernel(L0_attn_x, L0_attn_Wq, L0_attn_Wk, L0_attn_Wv, L0_attn_Wg, L0_attn_Wp,
           L0_ffn_W_up, L0_ffn_W_gate, L0_ffn_W_down):
    if "nc" not in _CACHE:
        _CACHE["nc"] = build()
    nc = _CACHE["nc"]
    X = np.ascontiguousarray(np.asarray(L0_attn_x, dtype=np.float32).reshape(NCORES * TOK, D))
    Wq = np.asarray(L0_attn_Wq, dtype=np.float32)
    Wk = np.asarray(L0_attn_Wk, dtype=np.float32)
    Wv = np.asarray(L0_attn_Wv, dtype=np.float32)
    Wg = np.asarray(L0_attn_Wg, dtype=np.float32)
    Wp = np.asarray(L0_attn_Wp, dtype=np.float32)
    Wu = np.asarray(L0_ffn_W_up, dtype=np.float32)
    Wg2 = np.asarray(L0_ffn_W_gate, dtype=np.float32)
    Wd = np.asarray(L0_ffn_W_down, dtype=np.float32)
    in_maps = []
    for c in range(NCORES):
        in_maps.append({
            "x_sh": np.ascontiguousarray(X[c * TOK:(c + 1) * TOK]),
            "wk_sh": np.ascontiguousarray(Wk[c * 256:(c + 1) * 256, :]),
            "wq_sh": np.ascontiguousarray(Wq[c * 256:(c + 1) * 256, :]),
            "wv_sh": np.ascontiguousarray(Wv[:, c * 256:(c + 1) * 256]),
            "wg_sh": np.ascontiguousarray(Wg[:, c * 256:(c + 1) * 256]),
            "wp_sh": np.ascontiguousarray(Wp[:, c * 256:(c + 1) * 256]),
            "kvmask": (np.arange(8) < c).astype(np.int32).reshape(1, 8),
            "wu_sh": np.ascontiguousarray(Wu[c * 1024:(c + 1) * 1024, :]),
            "wg2_sh": np.ascontiguousarray(Wg2[c * 1024:(c + 1) * 1024, :]),
            "wd_sh": np.ascontiguousarray(Wd[:, c * 1024:(c + 1) * 1024]),
        })
    res = run_bass_kernel_spmd(nc, in_maps, core_ids=list(range(NCORES)))
    out = np.concatenate([res.results[c]["y_sh"] for c in range(NCORES)], axis=0)
    return out.reshape(1, NCORES * TOK, D)


# version 04
# speedup vs baseline: 1.1547x; 1.1547x over previous
import numpy as np
import ml_dtypes
import concourse.bass as bass
import concourse.mybir as mybir
from concourse.bass_utils import run_bass_kernel_spmd
from contextlib import ExitStack

F32 = mybir.dt.float32
BF16 = mybir.dt.bfloat16
AF = mybir.ActivationFunctionType
ALU = mybir.AluOpType

NCORES = 8
TOK = 2048
D = 2048
ND = 16
NG = 4
GT = 512
NT = 16
NMAT = 5
EPS_LN = 1e-5
EPS_RMS = float(np.finfo(np.float32).eps)


def build(debug_outputs=True):
    nc = bass.Bass("TRN2", target_bir_lowering=False)

    x_sh = nc.dram_tensor("x_sh", [TOK, D], F32, kind="ExternalInput").ap()
    w_sh = {}
    w_sh["wk"] = nc.dram_tensor("wk_sh", [256, D], F32, kind="ExternalInput").ap()
    w_sh["wv"] = nc.dram_tensor("wv_sh", [D, 256], F32, kind="ExternalInput").ap()
    w_sh["wq"] = nc.dram_tensor("wq_sh", [256, D], F32, kind="ExternalInput").ap()
    w_sh["wg"] = nc.dram_tensor("wg_sh", [D, 256], F32, kind="ExternalInput").ap()
    w_sh["wp"] = nc.dram_tensor("wp_sh", [D, 256], F32, kind="ExternalInput").ap()
    kvmask = nc.dram_tensor("kvmask", [1, 8], mybir.dt.int32, kind="ExternalInput").ap()
    wu_sh = nc.dram_tensor("wu_sh", [1024, D], F32, kind="ExternalInput").ap()
    wg2_sh = nc.dram_tensor("wg2_sh", [1024, D], F32, kind="ExternalInput").ap()
    wd_sh = nc.dram_tensor("wd_sh", [D, 1024], F32, kind="ExternalInput").ap()

    y_sh = nc.dram_tensor("y_sh", [TOK, D], F32, kind="ExternalOutput").ap()

    MAT = {"wk": 0, "wv": 1, "wq": 2, "wg": 3, "wp": 4}
    aw_sh = nc.dram_tensor("aw_sh", [NMAT * 2 * 128, D], BF16).ap()
    aw_all = nc.dram_tensor("aw_all", [NCORES * NMAT * 2 * 128, D], BF16, addr_space="Shared").ap()

    def aw_img(m, s):
        row = ((s // 2) * (NMAT * 2) + m * 2 + (s % 2)) * 128
        return aw_all[row:row + 128, :]

    ktj_dram = nc.dram_tensor("ktj_dram", [NT * 128, D], BF16).ap()
    kvacc = nc.dram_tensor("kvacc", [D, D], F32).ap()
    kv_in = nc.dram_tensor("kv_in", [D, D], BF16).ap()
    kv_all = nc.dram_tensor("kv_all", [NCORES * D, D], BF16, addr_space="Shared").ap()
    kvp_dram = nc.dram_tensor("kvp_dram", [D, D], BF16).ap()
    x2_dram = nc.dram_tensor("x2_dram", [TOK, D], F32).ap()
    I8 = mybir.dt.int8
    ffn_sh = nc.dram_tensor("ffn_sh", [24 * 128, D], I8).ap()
    ffn_all = nc.dram_tensor("ffn_all", [NCORES * 24 * 128, D], I8, addr_space="Shared").ap()
    ar_in = nc.dram_tensor("ar_in", [1, 4], F32).ap()
    ar_out = nc.dram_tensor("ar_out", [1, 4], F32).ap()

    def ffn_img(m, blk):
        row = ((blk // 8) * 24 + m * 8 + (blk % 8)) * 128
        return ffn_all[row:row + 128, :]

    with ExitStack() as stk:
        e = stk.enter_context

        def sb(name, shape, dt):
            return e(nc.sbuf_tensor(name, shape, dt))

        def psum(name, shape, dt):
            return e(nc.psum_tensor(name, shape, dt))

        x_sb = sb("x_sb", [128, 2 * D], F32)
        xn_sb = sb("xn_sb", [128, 2 * D], BF16)
        xnT_g = sb("xnT_g", [128, ND * GT], BF16)
        v_sb = sb("v_sb", [128, ND * D], BF16)
        ktj_sb = sb("ktj_sb", [128, 4 * D], BF16)
        qT_g = ktj_sb
        stats = sb("stats", [128, NT * 8], F32)
        bnbuf = sb("bnbuf", [128, 24], F32)
        maggr = sb("maggr", [128, 2 * NT], F32)
        cs_all = sb("cs_all", [128, NT], F32)
        m_all = sb("m_all", [128, NT], F32)
        wo_buf = sb("wo_buf", [128, 2 * D], BF16)
        wt_buf = sb("wt_buf", [128, 4 * 512], BF16)

        idn = sb("idn", [128, 128], BF16)
        ktmp = sb("ktmp", [128, 2 * 128], BF16)
        masks = sb("masks", [128, 1024], BF16)
        g_sb = sb("g_sb", [128, 4 * D], BF16)
        sT_buf = xnT_g
        ogT_g = sb("ogT_g", [128, ND * GT], BF16)
        og_sb = sb("og_sb", [128, 1024], BF16)
        kvp_buf = sb("kvp_buf", [128, 2 * 1024], BF16)
        ktjs_buf = sb("ktjs_buf", [128, 2 * D], BF16)
        wp_buf = sb("wp_buf", [128, 2 * 1024], BF16)
        att_sb = sb("att_sb", [128, 2 * 1024], F32)
        acc_g = att_sb
        kvt_buf = ogT_g
        nat_sb = wp_buf
        img_sb_t = ktjs_buf
        mk8i = sb("mk8i", [1, 8], mybir.dt.int32)
        tern_sb = sb("tern_sb", [128, D], BF16)
        fpc_sb = sb("fpc_sb", [128, D], F32)
        fimg_sb = sb("fimg_sb", [128, D], mybir.dt.int8)
        absum = sb("absum", [128, 4], F32)
        svals = sb("svals", [1, 16], F32)
        scbc = sb("scbc", [128, 16], F32)
        ones_col = sb("ones_col", [128, 1], F32)

        pb = [psum(f"pb{i}", [128, 512], F32) for i in range(8)]
        TPA, TPB = 6, 7
        PKA, PKB = 4, 5

        class Sem:
            def __init__(self, name):
                self.h = e(nc.semaphore(name))
                self.n = 0

            def inc(self, inst, k=1):
                inst.then_inc(self.h, k)
                self.n += k
                return self.n

        s_ini = Sem("s_ini")
        s_pl = Sem("s_pl")
        s_mk = Sem("s_mk")
        s_po = Sem("s_po")
        s_cc = Sem("s_cc")
        s_x = [Sem("s_xA"), Sem("s_xB")]
        tile_ctr = [0]
        sv = Sem("sv")
        sc_ = Sem("sc")
        s_xn = Sem("s_xn")
        s_tp = Sem("s_tp")
        s_tpe = Sem("s_tpe")
        s_te = Sem("s_te")
        s_wo = [Sem("s_woA"), Sem("s_woB")]
        s_kp = Sem("s_kp")
        s_kev = Sem("s_kev")
        s_wt = [Sem(f"s_wt{i}") for i in range(4)]
        s_vd = Sem("s_vd")
        s_vev = Sem("s_vev")
        s_kmm = Sem("s_kmm")
        s_kve = Sem("s_kve")
        s_kvo = Sem("s_kvo")
        s_ks = Sem("s_ks")
        s_sum = Sem("s_sum")
        s_pu = Sem("s_pu")
        s_sm = [Sem("s_smA"), Sem("s_smB")]
        s_kvw = Sem("s_kvw")
        s_qe = Sem("s_qe")
        s_ge = Sem("s_ge")
        s_kj = [Sem("s_kjA"), Sem("s_kjB")]
        s_kjd = Sem("s_kjd")
        s_se = Sem("s_se")
        s_kvl = [Sem("s_kvlA"), Sem("s_kvlB")]
        s_kvd = Sem("s_kvd")
        s_og = Sem("s_og")
        s_ogt = Sem("s_ogt")
        s_oge = Sem("s_oge")
        s_wp = [Sem("s_wpA"), Sem("s_wpB")]
        s_wpd = Sem("s_wpd")
        s_aev = Sem("s_aev")
        s_acc = Sem("s_acc")
        s_x2i = Sem("s_x2i")
        s_fin = Sem("s_fin")
        s_scb = Sem("s_scb")
        s_ar2 = Sem("s_ar2")
        s_fb = Sem("s_fb")
        s_fl = Sem("s_fl")
        s_fl2 = Sem("s_fl2")
        s_fa = Sem("s_fa")
        s_ar = Sem("s_ar")
        s_tn = Sem("s_tn")
        s_fo = Sem("s_fo")
        s_fu = Sem("s_fu")
        s_fu2 = Sem("s_fu2")
        s_fg = Sem("s_fg")
        s_fg2 = Sem("s_fg2")
        s_fd = Sem("s_fd")
        s_fd2 = Sem("s_fd2")
        s_fud = Sem("s_fud")
        s_t1 = Sem("s_t1")
        s_hid = Sem("s_hid")
        s_dd = Sem("s_dd")
        s_fev = Sem("s_fev")
        s_yo = Sem("s_yo")
        s_x2l = [Sem(f"s_x2l{i}") for i in range(4)]

        block = e(nc.Block())

        def sec(eng, f):
            getattr(block, eng)(f)

        tp_units = [0]
        tile_marks = []

        def bank_guard(t):
            for s in (s_kev, s_se, s_vev, s_kve, s_oge, s_aev, s_tpe, s_og,
                      s_fev, s_hid, s_t1, s_dd, s_fud, s_scb):
                if s.n > 0:
                    t.wait_ge(s.h, s.n)

        def emit_transpose_unit(t, srcs, extra_wait=None):
            pg = tp_units[0]
            if pg >= 2:
                t.wait_ge(s_tpe.h, pg - 1)
            if extra_wait is not None:
                extra_wait(t)
            slot = pb[TPA + (pg % 2)][:, :].bitcast(BF16)
            last = None
            for idx, in_ap in enumerate(srcs):
                last = t.transpose(slot[:, idx * 128:(idx + 1) * 128], in_ap, idn[:, :])
            s_tp.inc(last)
            tp_units[0] += 1
            return pg

        def emit_evac_unit(v, pg, dsts):
            v.wait_ge(s_tp.h, pg + 1)
            slot = pb[TPA + (pg % 2)][:, :].bitcast(BF16)
            last = None
            for idx, out_ap in enumerate(dsts):
                last = v.tensor_copy(out_ap, slot[:, idx * 128:(idx + 1) * 128])
            s_tpe.inc(last)
            return last

        def _init(g):
            idn_f = x_sb[:, 0:128]
            g.iota(idn_f, pattern=[[-1, 128]], channel_multiplier=1,
                   allow_small_or_imprecise_dtypes=True)
            g.drain()
            g.tensor_single_scalar(idn[:, :], idn_f, 0.0, ALU.is_equal)
            g.drain()
            g.iota(x_sb[:, 0:1024], pattern=[[1, 1024]],
                   channel_multiplier=-1, base=-384,
                   allow_small_or_imprecise_dtypes=True)
            g.drain()
            last = g.tensor_single_scalar(masks[:, 0:1024], x_sb[:, 0:1024], 0.0, ALU.is_ge)
            g.memset(absum[:, :], 0.0)
            g.memset(ones_col[:, :], 1.0)
            s_ini.inc(last)

        sec("gpsimd", _init)

        def sy_x2i(s):
            s_x2i.inc(s.dma_start(out=x2_dram, in_=x_sh), 16)
        sec("sync", sy_x2i)

        def g_mk(g):
            s_mk.inc(g.dma_start(out=mk8i[0:1, :], in_=kvmask), 16)
        sec("gpsimd", g_mk)
        mk_pl_t = s_mk.n

        def emit_prep():
            for bi, (mname, sl) in enumerate(
                    [(m, s) for m in ["wk", "wv", "wq", "wg", "wp"] for s in range(2)]):
                m = MAT[mname]
                type_o = mname in ("wk", "wq")

                def g_load(g, mname=mname, sl=sl, bi=bi, type_o=type_o,
                           tp_before=tp_units[0]):
                    if bi > 0:
                        g.wait_ge(s_tp.h, tp_before)
                    if type_o:
                        s_pl.inc(g.dma_start(out=nat_sb[:, :],
                                             in_=w_sh[mname][sl * 128:(sl + 1) * 128, :]), 16)
                    else:
                        for r in range(16):
                            s_pl.inc(g.dma_start(
                                out=nat_sb[:, r * 128:(r + 1) * 128],
                                in_=w_sh[mname][r * 128:(r + 1) * 128, sl * 128:(sl + 1) * 128]), 16)
                sec("gpsimd", g_load)
                pl_target = s_pl.n

                pgs = []

                def t_tr(t, pl_target=pl_target, pgs=pgs):
                    def w0(tt):
                        tt.wait_ge(s_ini.h, 1)
                        tt.wait_ge(s_pl.h, pl_target)
                    for pr in range(8):
                        pgs.append(emit_transpose_unit(
                            t,
                            [nat_sb[:, (2 * pr) * 128:(2 * pr + 1) * 128],
                             nat_sb[:, (2 * pr + 1) * 128:(2 * pr + 2) * 128]],
                            extra_wait=w0 if pr == 0 else None))
                sec("tensor", t_tr)

                def v_ev(v, pgs=pgs, po_before=s_po.n, bi=bi):
                    if bi > 0:
                        v.wait_ge(s_po.h, po_before)
                    for pr in range(8):
                        emit_evac_unit(v, pgs[pr], [
                            img_sb_t[:, (2 * pr) * 128:(2 * pr + 1) * 128],
                            img_sb_t[:, (2 * pr + 1) * 128:(2 * pr + 2) * 128]])
                sec("vector", v_ev)

                def g_out(g, m=m, sl=sl, pe_target=s_tpe.n):
                    g.wait_ge(s_tpe.h, pe_target)
                    s_po.inc(g.dma_start(
                        out=aw_sh[(m * 2 + sl) * 128:(m * 2 + sl + 1) * 128, :],
                        in_=img_sb_t[:, 0:D]), 16)
                sec("gpsimd", g_out)

            def g_ag(g, po_target=s_po.n):
                g.wait_ge(s_po.h, po_target)
                s_cc.inc(g.collective_compute(
                    "AllGather", ALU.bypass, replica_groups=[list(range(NCORES))],
                    ins=[aw_sh], outs=[aw_all]))
            sec("gpsimd", g_ag)

        emit_prep()

        slab_f = g_sb[:, :].bitcast(F32)

        def slab_load(g, k, slot):
            if k < 8:
                g.dma_start(out=slab_f[:, slot * D:(slot + 1) * D],
                            in_=wu_sh[k * 128:(k + 1) * 128, :]).then_inc(
                                (s_fl if slot == 0 else s_fl2).h, 16)
            elif k < 16:
                r = k - 8
                g.dma_start(out=slab_f[:, slot * D:(slot + 1) * D],
                            in_=wg2_sh[r * 128:(r + 1) * 128, :]).then_inc(
                                (s_fl if slot == 0 else s_fl2).h, 16)
            else:
                s = k - 16
                for r in range(16):
                    g.dma_start(out=slab_f[:, slot * D + r * 128: slot * D + (r + 1) * 128],
                                in_=wd_sh[r * 128:(r + 1) * 128, s * 128:(s + 1) * 128]
                                ).then_inc((s_fl if slot == 0 else s_fl2).h, 16)
            (s_fl if slot == 0 else s_fl2).n += 16 * (1 if k < 16 else 16)

        def g_fpA_chunk(g, ks):
            for k in ks:
                if k >= 2:
                    g.wait_ge(s_fa.h, k - 1)
                slab_load(g, k, k % 2)
        fpA_marks = []
        _fl = [s_fl.n, s_fl2.n]
        _t0, _t1 = 0, 0
        for k in range(24):
            n = 16 * (1 if k < 16 else 16)
            if k % 2 == 0:
                _t0 += n
                fpA_marks.append((0, _t0))
            else:
                _t1 += n
                fpA_marks.append((1, _t1))

        def v_fpA_chunk(v, ks):
            if ks[0] == 0:
                v.wait_ge(s_ini.h, 1)
            for k in ks:
                slot, target = fpA_marks[k]
                v.wait_ge((s_fl if slot == 0 else s_fl2).h, target)
                m = 0 if k < 8 else (1 if k < 16 else 2)
                v.tensor_reduce(absum[:, 3:4], slab_f[:, slot * D:(slot + 1) * D],
                                mybir.AxisListType.X, ALU.add, apply_absolute_value=True)
                v.drain()
                s_fa.inc(v.tensor_add(absum[:, m:m + 1], absum[:, m:m + 1], absum[:, 3:4]))
                v.drain()

        def emit_xn_phase(g, with_stats):
            g4 = g * 4
            for i in range(4):
                ti = g4 + i
                n_tile = tile_ctr[0]
                tile_ctr[0] += 1
                xslot = n_tile % 2
                sxs = s_x[xslot]

                def sy_x(s, ti=ti, xslot=xslot, sxs=sxs, n_tile=n_tile):
                    if n_tile == 0:
                        s.wait_ge(s_ini.h, 1)
                    if s_xn.n >= 2:
                        s.wait_ge(s_xn.h, s_xn.n - 1)
                    sxs.inc(s.dma_start(out=x_sb[:, xslot * D:(xslot + 1) * D],
                                        in_=x_sh[ti * 128:(ti + 1) * 128, :]), 16)
                sec("sync", sy_x)
                x_sem = sxs
                x_target = sxs.n

                base = ti * 8
                col = lambda k, base=base: stats[:, base + k:base + k + 1]
                w1_c, sq1_c, r1_c, t_c, t2_c, w2_c, sq2_c, r2_c = [col(k) for k in range(8)]
                m_c = maggr[:, 2 * ti:2 * ti + 1]
                v_c = maggr[:, 2 * ti + 1:2 * ti + 2]

                if with_stats:
                    def v_stats(v, ti=ti, xslot=xslot, x_target=x_target,
                                w1_c=w1_c, sq1_c=sq1_c, r1_c=r1_c, t_c=t_c,
                                t2_c=t2_c, w2_c=w2_c, sq2_c=sq2_c, r2_c=r2_c,
                                m_c=m_c, v_c=v_c, n_tile=n_tile, x_sem=x_sem):
                        v.wait_ge(x_sem.h, x_target)
                        for cq in range(4):
                            v.bn_stats(bnbuf[:, cq * 6:(cq + 1) * 6],
                                       x_sb[:, xslot * D + cq * 512: xslot * D + (cq + 1) * 512])
                        v.drain()
                        v.bn_aggr(maggr[:, 2 * ti:2 * ti + 2], bnbuf[:, 0:24])
                        v.drain()
                        sv.inc(v.tensor_scalar_add(w1_c, v_c, EPS_LN))
                        v.wait_ge(sc_.h, sc_.n + 1)
                        v.reciprocal(r1_c, sq1_c)
                        v.drain()
                        v.tensor_mul(t_c, v_c, r1_c)
                        v.drain()
                        v.tensor_mul(t2_c, t_c, r1_c)
                        v.drain()
                        sv.inc(v.tensor_scalar_add(w2_c, t2_c, EPS_RMS))
                        v.wait_ge(sc_.h, sc_.n + 2)
                        v.reciprocal(r2_c, sq2_c)
                        v.drain()
                        v.tensor_mul(cs_all[:, ti:ti + 1], r1_c, r2_c)
                        v.tensor_copy(m_all[:, ti:ti + 1], maggr[:, 2 * ti:2 * ti + 1])
                        v.drain()
                        if n_tile >= 2:
                            v.wait_ge(s_tpe.h, tile_marks[n_tile - 2])
                        s_xn.inc(v.tensor_scalar(
                            xn_sb[:, (ti % 2) * D:(ti % 2 + 1) * D],
                            x_sb[:, xslot * D:(xslot + 1) * D],
                            m_c, cs_all[:, ti:ti + 1], ALU.subtract, ALU.mult))
                    sec("vector", v_stats)

                    def sc_chain(s, w1_c=w1_c, sq1_c=sq1_c, w2_c=w2_c, sq2_c=sq2_c,
                                 sv_base=sv.n):
                        s.wait_ge(sv.h, sv_base - 1)
                        sc_.inc(s.activation(sq1_c, w1_c, AF.Sqrt))
                        s.wait_ge(sv.h, sv_base)
                        sc_.inc(s.activation(sq2_c, w2_c, AF.Sqrt))
                    sec("scalar", sc_chain)
                else:
                    def v_norm(v, ti=ti, xslot=xslot, x_target=x_target, n_tile=n_tile,
                               x_sem=x_sem):
                        v.wait_ge(x_sem.h, x_target)
                        if n_tile >= 2:
                            v.wait_ge(s_tpe.h, tile_marks[n_tile - 2])
                        s_xn.inc(v.tensor_scalar(
                            xn_sb[:, (ti % 2) * D:(ti % 2 + 1) * D],
                            x_sb[:, xslot * D:(xslot + 1) * D],
                            m_all[:, ti:ti + 1], cs_all[:, ti:ti + 1],
                            ALU.subtract, ALU.mult))
                    sec("vector", v_norm)

                xn_target = s_xn.n
                pgs_x = []

                def t_xtr(t, ti=ti, pgs_x=pgs_x, xn_target=xn_target):
                    def w0(tt):
                        tt.wait_ge(s_xn.h, xn_target)
                    for pr in range(8):
                        pgs_x.append(emit_transpose_unit(
                            t,
                            [xn_sb[:, (ti % 2) * D + (2 * pr) * 128: (ti % 2) * D + (2 * pr + 1) * 128],
                             xn_sb[:, (ti % 2) * D + (2 * pr + 1) * 128: (ti % 2) * D + (2 * pr + 2) * 128]],
                            extra_wait=w0 if pr == 0 else None))
                sec("tensor", t_xtr)

                def v_xev(v, i=i, pgs_x=pgs_x):
                    for pr in range(8):
                        d0, d1 = 2 * pr, 2 * pr + 1
                        emit_evac_unit(v, pgs_x[pr], [
                            xnT_g[:, d0 * GT + i * 128: d0 * GT + (i + 1) * 128],
                            xnT_g[:, d1 * GT + i * 128: d1 * GT + (i + 1) * 128]])
                sec("vector", v_xev)
                tile_marks.append(s_tpe.n)

        def emit_typeO_proj(g, mname, act, evac_fn):
            m = MAT[mname]
            for o in range(16):
                u_kp = s_kp.n

                wos = s_wo[u_kp % 2]

                def g_w(gp, o=o, m=m, u_kp=u_kp, wos=wos):
                    if u_kp == 0:
                        gp.wait_ge(s_cc.h, 1)
                    if u_kp >= 2:
                        gp.wait_ge(s_kp.h, u_kp - 1)
                    wos.inc(gp.dma_start(
                        out=wo_buf[:, (u_kp % 2) * D:(u_kp % 2 + 1) * D],
                        in_=aw_img(m, o)), 16)
                sec("sync", g_w)

                def t_kp(t, o=o, u=u_kp, wos=wos, wo_target=wos.n,
                         first=(o == 0), tpe_t=s_tp.n):
                    if first:
                        bank_guard(t)
                        t.wait_ge(s_tpe.h, tpe_t)
                    t.wait_ge(wos.h, wo_target)
                    if u >= 2:
                        t.wait_ge(s_kev.h, u - 1)
                    pps = pb[PKA + (u % 2)]
                    mm = None
                    for d in range(ND):
                        mm = t.matmul(pps[:, :],
                                      wo_buf[:, (u % 2) * D + d * 128:(u % 2) * D + (d + 1) * 128],
                                      xnT_g[:, d * GT:(d + 1) * GT],
                                      start=(d == 0), stop=(d == ND - 1))
                    s_kp.inc(mm)
                sec("tensor", t_kp)

                def s_ev(s, o=o, u=u_kp):
                    s.wait_ge(s_kp.h, u + 1)
                    evac_fn(s, o, u, pb[PKA + (u % 2)])
                sec("scalar", s_ev)

        def emit_typeT_proj(g, mname, act, out_fn):
            m = MAT[mname]
            for eq in range(4):
                u_vp = (s_vd.n // 16)

                for d in range(ND):
                    w_wt = sum(s.n for s in s_wt) // 16
                    wts = s_wt[w_wt % 4]

                    def g_w(gp, m=m, eq=eq, d=d, w=w_wt, wts=wts):
                        if w >= 4:
                            gp.wait_ge(s_vd.h, w - 3)
                        wts.inc(gp.dma_start(
                            out=wt_buf[:, (w % 4) * 512:(w % 4 + 1) * 512],
                            in_=aw_img(m, d)[:, eq * 512:(eq + 1) * 512]), 16)
                    sec("sync", g_w)

                    def t_vp(t, eq=eq, d=d, w=w_wt, u_vp=u_vp, wts=wts, wt_target=wts.n,
                             vev_before=s_vev.n, first=(eq == 0 and d == 0), tpe_t=s_tp.n):
                        if first:
                            bank_guard(t)
                            t.wait_ge(s_tpe.h, tpe_t)
                        t.wait_ge(wts.h, wt_target)
                        if d == 0 and u_vp >= 1:
                            t.wait_ge(s_vev.h, vev_before)
                        mm = None
                        for i in range(4):
                            mm = t.matmul(pb[i][:, :],
                                          xnT_g[:, d * GT + i * 128: d * GT + (i + 1) * 128],
                                          wt_buf[:, (w % 4) * 512:(w % 4 + 1) * 512],
                                          start=(d == 0), stop=(d == ND - 1))
                        s_vd.inc(mm)
                    sec("tensor", t_vp)

                def s_ev(s, eq=eq, act=act, out_fn=out_fn, vd_t=s_vd.n):
                    s.wait_ge(s_vd.h, vd_t)
                    for i in range(4):
                        s_vev.inc(s.activation(out_fn(i, eq), pb[i][:, :], act))
                sec("scalar", s_ev)

        for g in range(NG):
            sec("gpsimd", lambda gp, ks=list(range(g * 6, g * 6 + 6)): g_fpA_chunk(gp, ks))
            emit_xn_phase(g, with_stats=True)
            sec("vector", lambda v, ks=list(range(g * 6, g * 6 + 6)): v_fpA_chunk(v, ks))

            ks_before = s_ks.n

            def k_evac(s, o, u, pps, g=g, ks_before=ks_before):
                if o == 0 and ks_before > 0:
                    s.wait_ge(s_ks.h, ks_before)
                last = None
                for i in range(4):
                    last = s.activation(ktj_sb[:, i * D + o * 128: i * D + (o + 1) * 128],
                                        pps[:, i * 128:(i + 1) * 128], AF.Silu)
                s_kev.inc(last)
            emit_typeO_proj(g, "wk", AF.Silu, k_evac)

            emit_typeT_proj(g, "wv", AF.Silu,
                            lambda i, eq, g=g: v_sb[:, (g * 4 + i) * D + eq * 512:
                                                    (g * 4 + i) * D + (eq + 1) * 512])

            kev_t = s_kev.n
            vev_t = s_vev.n
            kv_pgs = {}
            for o in range(16):
                for j in range(4):
                    kj = g * 64 + o * 4 + j

                    def t_kv1(t, g=g, o=o, j=j, kj=kj, kev_t=kev_t, vev_t=vev_t):
                        def w0(tt):
                            if o == 0 and j == 0:
                                bank_guard(tt)
                                tt.wait_ge(s_kev.h, kev_t)
                                tt.wait_ge(s_vev.h, vev_t)
                            if o >= 1 and j == 0:
                                tt.wait_ge(s_kve.h, g * 16 + o)
                        if kj % 64 == 0:
                            kv_pgs[kj] = emit_transpose_unit(
                                t, [ktj_sb[:, j * D + o * 128: j * D + (o + 1) * 128]],
                                extra_wait=w0)
                        if (kj + 1) % 64 != 0:
                            o2, j2 = (o * 4 + j + 1) // 4, (o * 4 + j + 1) % 4
                            kv_pgs[kj + 1] = emit_transpose_unit(
                                t, [ktj_sb[:, j2 * D + o2 * 128: j2 * D + (o2 + 1) * 128]],
                                extra_wait=None)
                        pg = kv_pgs[kj]
                        t.wait_ge(s_tpe.h, pg + 1)
                        mm = None
                        for er in range(4):
                            mm = t.matmul(pb[er][:, :],
                                          ktmp[:, (kj % 2) * 128:(kj % 2 + 1) * 128],
                                          v_sb[:, (g * 4 + j) * D + er * 512:
                                               (g * 4 + j) * D + (er + 1) * 512],
                                          start=(j == 0), stop=(j == 3))
                        s_kmm.inc(mm)
                    sec("tensor", t_kv1)

                    def v_kte(v, kj=kj):
                        if kj >= 2:
                            v.wait_ge(s_kmm.h, kj - 1)
                        emit_evac_unit(v, kv_pgs[kj],
                                       [ktmp[:, (kj % 2) * 128:(kj % 2 + 1) * 128]])
                    sec("vector", v_kte)

                def v_kve(v, g=g, o=o, kvo_before=s_kvo.n):
                    v.wait_ge(s_kmm.h, (g * 16 + o + 1) * 4)
                    if kvo_before >= 16:
                        v.wait_ge(s_kvo.h, kvo_before)
                    last = None
                    for er in range(4):
                        last = v.tensor_copy(fpc_sb[:, er * 512:(er + 1) * 512], pb[er][:, :])
                    s_kve.inc(last)
                sec("vector", v_kve)

                def sy_kv(s, g=g, o=o):
                    s.wait_ge(s_kve.h, g * 16 + o + 1)
                    if g == 0:
                        s_kvo.inc(s.dma_start(out=kvacc[o * 128:(o + 1) * 128, :],
                                              in_=fpc_sb[:, :]), 16)
                    else:
                        s_kvo.inc(s.dma_start(out=kvacc[o * 128:(o + 1) * 128, :],
                                              in_=fpc_sb[:, :], accum_op=ALU.add), 16)
                    s.wait_ge(s_kvo.h, s_kvo.n)
                sec("gpsimd", sy_kv)

            def sy_ks(s, g=g, kmm_t=s_kmm.n):
                s.wait_ge(s_kmm.h, kmm_t)
                for j in range(4):
                    s_ks.inc(s.dma_start(
                        out=ktj_dram[(g * 4 + j) * 128:(g * 4 + j + 1) * 128, :],
                        in_=ktj_sb[:, j * D:(j + 1) * D]), 16)
            sec("sync", sy_ks)


        def t_fpB1(t, fa_t=s_fa.n, ini_t=1):
            bank_guard(t)
            t.wait_ge(s_ini.h, ini_t)
            t.wait_ge(s_fa.h, fa_t)
            s_fb.inc(t.matmul(pb[PKA][0:4, 0:1], absum[:, 0:4], ones_col[:, :],
                              start=True, stop=True))
        sec("tensor", t_fpB1)
        fpB1_kp = s_fb.n

        def v_fpB1(v, kp_t=fpB1_kp):
            v.wait_ge(s_fb.h, kp_t)
            s_fa.inc(v.tensor_copy(absum[0:4, 3:4], pb[PKA][0:4, 0:1]))
        sec("vector", v_fpB1)
        fpB1_fa = s_fa.n

        onesf = ktmp[:, :].bitcast(F32)[0:1, 0:128]

        def g_fpB(g, fa_t=fpB1_fa, kvo_t=s_kvo.n):
            g.wait_ge(s_fa.h, fa_t)
            g.wait_ge(s_kvo.h, kvo_t)
            g.memset(onesf, 1.0)
            g.drain()
            g.dma_start(out=ar_in, in_=absum[0:4, 3:4]).then_inc(s_ar.h, 16)
            s_ar.n += 16
            g.wait_ge(s_ar.h, s_ar.n)
            s_cc.inc(g.collective_compute(
                "AllReduce", ALU.add, replica_groups=[list(range(NCORES))],
                ins=[ar_in], outs=[ar_out]))
            g.wait_ge(s_cc.h, 2)
            g.dma_start(out=svals[0:1, 0:4], in_=ar_out).then_inc(s_ar.h, 16)
            s_ar.n += 16
        sec("gpsimd", g_fpB)
        ar_t = s_ar.n

        def v_fpB2(v, ar_t=ar_t):
            v.wait_ge(s_ar.h, ar_t)
            v.tensor_scalar(svals[0:1, 4:7], svals[0:1, 0:3], 2.0 ** -24, 1e-8,
                            ALU.mult, ALU.add)
            v.drain()
            v.tensor_mul(svals[0:1, 7:8], svals[0:1, 5:6], svals[0:1, 6:7])
            v.tensor_scalar_mul(svals[0:1, 8:11], svals[0:1, 4:7], 0.5)
            v.drain()
            v.tensor_scalar_mul(svals[0:1, 11:14], svals[0:1, 8:11], -1.0)
            v.tensor_scalar_mul(svals[0:1, 14:15], svals[0:1, 4:5], -1.0)
            v.tensor_scalar_mul(svals[0:1, 15:16], svals[0:1, 7:8], -1.0)
            s_ar2.inc(v.drain())
        sec("vector", v_fpB2)

        def t_fpB2(t, ar2_t=s_ar2.n):
            bank_guard(t)
            t.wait_ge(s_ar2.h, ar2_t)
            s_fb.inc(t.matmul(pb[PKB][:, 0:16], onesf[0:1, :], svals[0:1, 0:16],
                              start=True, stop=True))
        sec("tensor", t_fpB2)
        fpB2_kp = s_fb.n

        def v_fpB3(v, kp_t=fpB2_kp):
            v.wait_ge(s_fb.h, kp_t)
            s_scb.inc(v.tensor_copy(scbc[:, :], pb[PKB][:, 0:16]))
        sec("vector", v_fpB3)

        def g_kvsum(g, kvo_t=s_kvo.n, mk_pl_t=mk_pl_t):
            kvt_f = ogT_g[:, :].bitcast(F32)
            g.wait_ge(s_kvo.h, kvo_t)
            g.wait_ge(s_mk.h, mk_pl_t)
            g.dma_start(out=kv_in, in_=kvacc).then_inc(s_sum.h, 16)
            s_sum.n += 16
            g.wait_ge(s_sum.h, s_sum.n)
            s_cc.inc(g.collective_compute(
                "AllGather", ALU.bypass, replica_groups=[list(range(NCORES))],
                ins=[kv_in], outs=[kv_all]))
            g.wait_ge(s_cc.h, 3)
            with g.register("mreg") as mreg:
                for ds in range(16):
                    g.memset(acc_g[:, :], 0.0)
                    g.drain()
                    base_units = s_pu.n

                    def load(jj):
                        sm = s_sm[jj % 2]
                        if jj >= 2:
                            g.wait_ge(s_pu.h, base_units + jj - 1)
                        g.dma_start(out=kvt_f[:, (jj % 2) * D:(jj % 2 + 1) * D],
                                    in_=kv_all[(jj * 16 + ds) * 128:
                                               (jj * 16 + ds + 1) * 128, :]
                                    ).then_inc(sm.h, 16)
                        sm.n += 16

                    load(0)
                    for jj in range(8):
                        if jj + 1 < 8:
                            load(jj + 1)
                        sm = s_sm[jj % 2]
                        g.wait_ge(sm.h, sm.n)
                        g.reg_load(mreg, mk8i[0:1, jj:jj + 1])
                        with g.If_eq(mreg, 1):
                            g.tensor_add(acc_g[:, :], acc_g[:, :],
                                         kvt_f[:, (jj % 2) * D:(jj % 2 + 1) * D]
                                         ).then_inc(s_pu.h, 1)
                        with g.Else():
                            g.sem_inc(s_pu.h, 1)
                        s_pu.n += 1
                    g.drain()
                    g.tensor_copy(kvt_f[:, 0:D].bitcast(BF16)[:, 0:D], acc_g[:, :])
                    g.drain()
                    g.dma_start(out=kvp_dram[ds * 128:(ds + 1) * 128, :],
                                in_=kvt_f[:, 0:D].bitcast(BF16)[:, 0:D]).then_inc(s_kvw.h, 16)
                    s_kvw.n += 16
                    g.wait_ge(s_kvw.h, s_kvw.n)
        sec("gpsimd", g_kvsum)

        s_fc = Sem("s_fc")
        s_tn2 = Sem("s_tn2")
        fc_marks = {}

        def g_fpC_load(gp, k):
            if k >= 1:
                gp.wait_ge(s_tn.h, k)
            if k < 8:
                s_fc.inc(gp.dma_start(out=fpc_sb[:, :],
                                      in_=wu_sh[k * 128:(k + 1) * 128, :]), 16)
            elif k < 16:
                r = k - 8
                s_fc.inc(gp.dma_start(out=fpc_sb[:, :],
                                      in_=wg2_sh[r * 128:(r + 1) * 128, :]), 16)
            else:
                s = k - 16
                for r in range(16):
                    s_fc.inc(gp.dma_start(
                        out=fpc_sb[:, r * 128:(r + 1) * 128],
                        in_=wd_sh[r * 128:(r + 1) * 128, s * 128:(s + 1) * 128]), 16)
            fc_marks[k] = s_fc.n

        def v_fpC_tern(v, k, scb_t):
            v.wait_ge(s_fc.h, fc_marks[k])
            v.wait_ge(s_scb.h, scb_t)
            m = 0 if k < 8 else (1 if k < 16 else 2)
            v.tensor_single_scalar(tern_sb[:, :], fpc_sb[:, :],
                                   scbc[:, 8 + m:9 + m], ALU.is_gt)
            v.drain()
            s_tn.inc(v.scalar_tensor_tensor(tern_sb[:, :], fpc_sb[:, :],
                                            scbc[:, 11 + m:12 + m], tern_sb[:, :],
                                            ALU.is_lt, ALU.subtract))

        def emit_fpC_slab(k, scb_t):
            sec("gpsimd", lambda gp, k=k: g_fpC_load(gp, k))
            sec("vector", lambda v, k=k, scb_t=scb_t: v_fpC_tern(v, k, scb_t))
            pgs_f = []

            def t_ftr(t, k=k, pgs_f=pgs_f, tn_t=s_tn.n):
                def w0(tt):
                    tt.wait_ge(s_tn.h, tn_t)
                for pr in range(8):
                    pgs_f.append(emit_transpose_unit(
                        t,
                        [tern_sb[:, (2 * pr) * 128:(2 * pr + 1) * 128],
                         tern_sb[:, (2 * pr + 1) * 128:(2 * pr + 2) * 128]],
                        extra_wait=w0 if pr == 0 else None))
            sec("tensor", t_ftr)

            def v_fev2(v, k=k, pgs_f=pgs_f, fo_t=16 * k):
                if fo_t > 0:
                    v.wait_ge(s_fo.h, fo_t)
                for pr in range(8):
                    emit_evac_unit(v, pgs_f[pr], [
                        fimg_sb[:, (2 * pr) * 128:(2 * pr + 1) * 128],
                        fimg_sb[:, (2 * pr + 1) * 128:(2 * pr + 2) * 128]])
            sec("vector", v_fev2)

            def g_fpC_out(gp, k=k, tpe_t=s_tpe.n):
                gp.wait_ge(s_tpe.h, tpe_t)
                s_fo.inc(gp.dma_start(out=ffn_sh[k * 128:(k + 1) * 128, :],
                                      in_=fimg_sb[:, :]), 16)
            sec("gpsimd", g_fpC_out)

        for g in range(NG):
            emit_xn_phase(g, with_stats=False)

            def q_evac(s, o, u, pps, ks_t=s_ks.n):
                if o == 0:
                    s.wait_ge(s_ks.h, ks_t)
                s_kev.inc(s.activation(qT_g[:, o * GT:(o + 1) * GT], pps[:, :], AF.Silu))
            emit_typeO_proj(g, "wq", AF.Silu, q_evac)

            emit_typeT_proj(g, "wg", AF.Sigmoid,
                            lambda i, eq: g_sb[:, i * D + eq * 512: i * D + (eq + 1) * 512])

            njt = 4 * (g + 1)
            qe_t = s_kev.n
            for jt in range(njt):
                u_kj = (s_kj[0].n + s_kj[1].n) // 16
                kjs = s_kj[u_kj % 2]

                def sy_kj(s, jt=jt, u=u_kj, po_t=s_po.n, kjs=kjs):
                    if u == 0:
                        s.wait_ge(s_po.h, po_t)
                    if u >= 2:
                        s.wait_ge(s_kjd.h, u - 1)
                    kjs.inc(s.dma_start(out=ktjs_buf[:, (u % 2) * D:(u % 2 + 1) * D],
                                        in_=ktj_dram[jt * 128:(jt + 1) * 128, :]), 16)
                sec("sync", sy_kj)

                def t_sT(t, g=g, jt=jt, u=u_kj, kjs=kjs, kj_target=kjs.n, qe_t=qe_t,
                         u_sT=s_se.n, ks_t=s_ks.n):
                    if jt == 0:
                        bank_guard(t)
                        t.wait_ge(s_kev.h, qe_t)
                        t.wait_ge(s_ks.h, ks_t)
                    t.wait_ge(kjs.h, kj_target)
                    if u_sT >= 2:
                        t.wait_ge(s_se.h, u_sT - 1)
                    pps = pb[PKA + (u_sT % 2)]
                    mm = None
                    for ft in range(ND):
                        mm = t.matmul(pps[:, :],
                                      ktjs_buf[:, (u % 2) * D + ft * 128:(u % 2) * D + (ft + 1) * 128],
                                      qT_g[:, ft * GT:(ft + 1) * GT],
                                      start=(ft == 0), stop=(ft == ND - 1))
                    s_kjd.inc(mm)
                sec("tensor", t_sT)

                def v_se(v, g=g, jt=jt, u_sT=s_se.n, kjd_t=s_kjd.n):
                    v.wait_ge(s_kjd.h, kjd_t)
                    pps = pb[PKA + (u_sT % 2)]
                    r = jt - 4 * g
                    if r >= 0:
                        s_se.inc(v.tensor_mul(sT_buf[:, jt * GT:(jt + 1) * GT],
                                              pps[:, :],
                                              masks[:, 384 - 128 * r: 896 - 128 * r]))
                    else:
                        s_se.inc(v.tensor_copy(sT_buf[:, jt * GT:(jt + 1) * GT], pps[:, :]))
                sec("vector", v_se)

            if g <= 1:
                for k in range(g * 12, g * 12 + 12):
                    emit_fpC_slab(k, 1)
            if g == 1:
                def g_fAG(gp, fo_t=s_fo.n):
                    gp.wait_ge(s_fo.h, fo_t)
                    s_cc.inc(gp.collective_compute(
                        "AllGather", ALU.bypass, replica_groups=[list(range(NCORES))],
                        ins=[ffn_sh], outs=[ffn_all]))
                sec("gpsimd", g_fAG)

            se_t = s_se.n
            ge_t = s_vev.n
            for half in range(2):
                kv_sems = []
                for d in range(ND):
                    u_kv = (s_kvl[0].n + s_kvl[1].n) // 16
                    kvs = s_kvl[u_kv % 2]

                    def sy_kvl(s, g=g, half=half, d=d, u=u_kv, kvw_t=s_kvw.n, kvs=kvs):
                        if u == 0:
                            s.wait_ge(s_kvw.h, kvw_t)
                        if u >= 2:
                            s.wait_ge(s_kvd.h, u - 1)
                        kvs.inc(s.dma_start(
                            out=kvp_buf[:, (u % 2) * 1024:(u % 2 + 1) * 1024],
                            in_=kvp_dram[d * 128:(d + 1) * 128, half * 1024:(half + 1) * 1024]), 16)
                    sec("sync", sy_kvl)
                    kv_sems.append((kvs, kvs.n))

                def t_o(t, g=g, half=half, njt=njt, se_t=se_t, ge_t=ge_t,
                        kv_sems=kv_sems):
                    bank_guard(t)
                    t.wait_ge(s_se.h, se_t)
                    t.wait_ge(s_vev.h, ge_t)
                    for i in range(4):
                        first = True
                        for jt in range(4 * g + i + 1):
                            for er in range(2):
                                t.matmul(pb[2 * i + er][:, :],
                                         sT_buf[:, jt * GT + i * 128: jt * GT + (i + 1) * 128],
                                         v_sb[:, jt * D + half * 1024 + er * 512:
                                              jt * D + half * 1024 + (er + 1) * 512],
                                         start=(jt == 0), stop=False)
                    for d in range(ND):
                        t.wait_ge(kv_sems[d][0].h, kv_sems[d][1])
                        mm = None
                        for i in range(4):
                            for er in range(2):
                                mm = t.matmul(pb[2 * i + er][:, :],
                                              qT_g[:, d * GT + i * 128: d * GT + (i + 1) * 128],
                                              kvp_buf[:, (d % 2) * 1024 + er * 512:
                                                      (d % 2) * 1024 + (er + 1) * 512],
                                              start=False, stop=(d == ND - 1))
                        s_kvd.inc(mm)
                sec("tensor", t_o)

                for i in range(4):
                    def v_og(v, g=g, half=half, i=i, kvd_t=s_kvd.n,
                             ogt_before=s_ogt.n):
                        v.wait_ge(s_kvd.h, kvd_t)
                        if ogt_before > 0:
                            v.wait_ge(s_ogt.h, ogt_before)
                        last = None
                        for er in range(2):
                            last = v.tensor_mul(
                                og_sb[:, er * 512:(er + 1) * 512],
                                pb[2 * i + er][:, :],
                                g_sb[:, i * D + half * 1024 + er * 512:
                                     i * D + half * 1024 + (er + 1) * 512])
                        s_og.inc(last)
                    sec("vector", v_og)

                    def t_ogt(t, g=g, half=half, i=i, og_t=s_og.n):
                        t.wait_ge(s_og.h, og_t)
                        last = None
                        for et in range(8):
                            bk = pb[2 * i + et // 4][:, :].bitcast(BF16)
                            last = t.transpose(
                                bk[:, (et % 4) * 128:(et % 4 + 1) * 128],
                                og_sb[:, et * 128:(et + 1) * 128], idn[:, :])
                        s_ogt.inc(last)
                    sec("tensor", t_ogt)

                    def v_oge(v, g=g, half=half, i=i, ogt_t=s_ogt.n):
                        v.wait_ge(s_ogt.h, ogt_t)
                        last = None
                        for et in range(8):
                            eg = half * 8 + et
                            bk = pb[2 * i + et // 4][:, :].bitcast(BF16)
                            last = v.tensor_copy(
                                ogT_g[:, eg * GT + i * 128: eg * GT + (i + 1) * 128],
                                bk[:, (et % 4) * 128:(et % 4 + 1) * 128])
                        s_oge.inc(last)
                    sec("vector", v_oge)

            oge_t = s_oge.n
            for dmh in range(2):
                wp_sems = []
                for et in range(ND):
                    u_wp = (s_wp[0].n + s_wp[1].n) // 16
                    wps = s_wp[u_wp % 2]

                    def sy_wp(s, g=g, dmh=dmh, et=et, u=u_wp, wps=wps):
                        if u >= 2:
                            s.wait_ge(s_wpd.h, u - 1)
                        wps.inc(s.dma_start(
                            out=wp_buf[:, (u % 2) * 1024:(u % 2 + 1) * 1024],
                            in_=aw_img(MAT["wp"], et)[:, dmh * 1024:(dmh + 1) * 1024]), 16)
                    sec("sync", sy_wp)
                    wp_sems.append((wps, wps.n))

                def t_wp(t, g=g, dmh=dmh, oge_t=oge_t, wp_sems=wp_sems,
                         aev_before=s_aev.n):
                    bank_guard(t)
                    t.wait_ge(s_oge.h, oge_t)
                    if aev_before >= 1:
                        t.wait_ge(s_aev.h, aev_before)
                    for et in range(ND):
                        t.wait_ge(wp_sems[et][0].h, wp_sems[et][1])
                        mm = None
                        for i in range(4):
                            for dr in range(2):
                                mm = t.matmul(
                                    pb[2 * i + dr][:, :],
                                    ogT_g[:, et * GT + i * 128: et * GT + (i + 1) * 128],
                                    wp_buf[:, (et % 2) * 1024 + dr * 512:
                                           (et % 2) * 1024 + (dr + 1) * 512],
                                    start=(et == 0), stop=(et == ND - 1))
                        s_wpd.inc(mm)
                sec("tensor", t_wp)

                for i in range(4):
                    def v_aev(v, g=g, dmh=dmh, i=i, wpd_t=s_wpd.n,
                              acc_before=s_acc.n):
                        v.wait_ge(s_wpd.h, wpd_t)
                        if acc_before >= 32:
                            v.wait_ge(s_acc.h, acc_before - 16)
                        last = None
                        for dr in range(2):
                            last = v.tensor_copy(
                                att_sb[:, (i % 2) * 1024 + dr * 512:
                                       (i % 2) * 1024 + (dr + 1) * 512],
                                pb[2 * i + dr][:, :])
                        s_aev.inc(last)
                    sec("vector", v_aev)

                    def g_acc(gp, g=g, dmh=dmh, i=i, aev_t=s_aev.n, x2i_t=s_x2i.n):
                        gp.wait_ge(s_aev.h, aev_t)
                        gp.wait_ge(s_x2i.h, x2i_t)
                        s_acc.inc(gp.dma_start(
                            out=x2_dram[(g * 4 + i) * 128:(g * 4 + i + 1) * 128,
                                        dmh * 1024:(dmh + 1) * 1024],
                            in_=att_sb[:, (i % 2) * 1024:(i % 2 + 1) * 1024],
                            accum_op=ALU.add), 16)
                        gp.wait_ge(s_acc.h, s_acc.n)
                    sec("gpsimd", g_acc)

        x2g_f = [ktj_sb[:, :].bitcast(F32), ogT_g[:, :].bitcast(F32)]

        def x2tile(i):
            return x2g_f[i // 2][:, (i % 2) * D:(i % 2 + 1) * D]

        xn2T = xnT_g
        hidT = v_sb
        t1v = kvp_buf[:, :].bitcast(F32)
        wd_buf2 = wp_buf
        out_sb = att_sb
        s_yo2 = Sem("s_yo2")
        acc_total = s_acc.n

        for g3 in range(NG):
            g4 = g3 * 4
            for i in range(4):
                ti = g4 + i
                n_tile = tile_ctr[0]
                tile_ctr[0] += 1
                sxs = s_x2l[i]

                def sy_x2(s, ti=ti, i=i, g3=g3, sxs=sxs, fev_t=8 * g3):
                    s.wait_ge(s_acc.h, acc_total)
                    if fev_t > 0:
                        s.wait_ge(s_fev.h, fev_t)
                    sxs.inc(s.dma_start(out=x2tile(i),
                                        in_=x2_dram[ti * 128:(ti + 1) * 128, :]), 16)
                sec("sync", sy_x2)
                x2_target = sxs.n

                base = ti * 8
                col = lambda k, base=base: stats[:, base + k:base + k + 1]
                w1_c, sq1_c, r1_c = col(0), col(1), col(2)
                m_c = maggr[:, 2 * ti:2 * ti + 1]
                v_c = maggr[:, 2 * ti + 1:2 * ti + 2]

                def v_stats2(v, ti=ti, i=i, x2_target=x2_target, sxs=sxs,
                             w1_c=w1_c, sq1_c=sq1_c, r1_c=r1_c, m_c=m_c, v_c=v_c,
                             n_tile=n_tile):
                    v.wait_ge(sxs.h, x2_target)
                    for cq in range(4):
                        v.bn_stats(bnbuf[:, cq * 6:(cq + 1) * 6],
                                   x2tile(i)[:, cq * 512:(cq + 1) * 512])
                    v.drain()
                    v.bn_aggr(maggr[:, 2 * ti:2 * ti + 2], bnbuf[:, 0:24])
                    v.drain()
                    sv.inc(v.tensor_scalar_add(w1_c, v_c, EPS_LN))
                    v.wait_ge(sc_.h, sc_.n + 1)
                    v.reciprocal(r1_c, sq1_c)
                    v.drain()
                    v.tensor_copy(cs_all[:, ti:ti + 1], r1_c)
                    v.drain()
                    if n_tile >= 2:
                        v.wait_ge(s_tpe.h, tile_marks[n_tile - 2])
                    s_xn.inc(v.tensor_scalar(
                        xn_sb[:, (n_tile % 2) * D:(n_tile % 2 + 1) * D],
                        x2tile(i),
                        m_c, cs_all[:, ti:ti + 1], ALU.subtract, ALU.mult))
                sec("vector", v_stats2)

                def sc_chain2(s, w1_c=w1_c, sq1_c=sq1_c, sv_base=sv.n):
                    s.wait_ge(sv.h, sv_base)
                    sc_.inc(s.activation(sq1_c, w1_c, AF.Sqrt))
                sec("scalar", sc_chain2)

                xn_target = s_xn.n
                pgs_x = []

                def t_xtr2(t, n_tile=n_tile, pgs_x=pgs_x, xn_target=xn_target,
                           first=(i == 0)):
                    def w0(tt):
                        if first:
                            bank_guard(tt)
                        tt.wait_ge(s_xn.h, xn_target)
                    for pr in range(8):
                        pgs_x.append(emit_transpose_unit(
                            t,
                            [xn_sb[:, (n_tile % 2) * D + (2 * pr) * 128:
                                   (n_tile % 2) * D + (2 * pr + 1) * 128],
                             xn_sb[:, (n_tile % 2) * D + (2 * pr + 1) * 128:
                                   (n_tile % 2) * D + (2 * pr + 2) * 128]],
                            extra_wait=w0 if pr == 0 else None))
                sec("tensor", t_xtr2)

                def v_xev2(v, i=i, pgs_x=pgs_x):
                    for pr in range(8):
                        d0, d1 = 2 * pr, 2 * pr + 1
                        emit_evac_unit(v, pgs_x[pr], [
                            xn2T[:, d0 * GT + i * 128: d0 * GT + (i + 1) * 128],
                            xn2T[:, d1 * GT + i * 128: d1 * GT + (i + 1) * 128]])
                sec("vector", v_xev2)
                tile_marks.append(s_tpe.n)

            tpe_now = s_tp.n
            for b in range(64):
                ub = g3 * 64 + b
                slot = ub % 2
                fus = [s_fu, s_fu2][slot]
                fgs = [s_fg, s_fg2][slot]

                def g_ld_ug(gp, b=b, ub=ub, slot=slot, fus=fus, fgs=fgs, cc_t=4):
                    if ub == 0:
                        gp.wait_ge(s_cc.h, cc_t)
                    if ub >= 2:
                        gp.wait_ge(s_hid.h, ub - 1)
                    fus.inc(gp.dma_start(out=wo_buf[:, slot * D:(slot + 1) * D],
                                         in_=ffn_img(0, b)), 16)
                    fgs.inc(gp.dma_start(out=ktjs_buf[:, slot * D:(slot + 1) * D],
                                         in_=ffn_img(1, b)), 16)
                sec("gpsimd", g_ld_ug)
                fu_t, fg_t = fus.n, fgs.n

                def t_ug(t, b=b, ub=ub, slot=slot, fus=fus, fgs=fgs,
                         fu_t=fu_t, fg_t=fg_t, g3=g3, tpe_now=tpe_now,
                         first=(b == 0)):
                    if first:
                        bank_guard(t)
                        t.wait_ge(s_tpe.h, tpe_now)
                    t.wait_ge(fus.h, fu_t)
                    t.wait_ge(fgs.h, fg_t)
                    if ub >= 2:
                        t.wait_ge(s_hid.h, ub - 1)
                    mm = None
                    for d in range(ND):
                        t.matmul(pb[2 * slot][:, :],
                                 wo_buf[:, slot * D + d * 128: slot * D + (d + 1) * 128],
                                 xn2T[:, d * GT:(d + 1) * GT],
                                 start=(d == 0), stop=(d == ND - 1))
                    for d in range(ND):
                        mm = t.matmul(pb[2 * slot + 1][:, :],
                                      ktjs_buf[:, slot * D + d * 128: slot * D + (d + 1) * 128],
                                      xn2T[:, d * GT:(d + 1) * GT],
                                      start=(d == 0), stop=(d == ND - 1))
                    s_fud.inc(mm)
                sec("tensor", t_ug)

                def s_t1f(s, ub=ub, slot=slot, fud_t=s_fud.n, scb_t=1):
                    s.wait_ge(s_fud.h, fud_t)
                    if ub == 0:
                        s.wait_ge(s_scb.h, scb_t)
                    s_t1.inc(s.activation(t1v[:, slot * 512:(slot + 1) * 512],
                                          pb[2 * slot][:, :], AF.Silu,
                                          scale=scbc[:, 14:15]))
                sec("scalar", s_t1f)

                def v_hid(v, b=b, ub=ub, slot=slot, g3=g3, t1_t=s_t1.n):
                    v.wait_ge(s_t1.h, t1_t)
                    s_hid.inc(v.scalar_tensor_tensor(
                        hidT[:, b * 512:(b + 1) * 512],
                        pb[2 * slot + 1][:, :], scbc[:, 15:16],
                        t1v[:, slot * 512:(slot + 1) * 512],
                        ALU.mult, ALU.mult))
                sec("vector", v_hid)

            for dmh in range(2):
                fd_sems = []
                for e8 in range(64):
                    ud = (s_fd.n + s_fd2.n) // 16
                    fds = [s_fd, s_fd2][ud % 2]

                    def g_ld_d(gp, e8=e8, dmh=dmh, ud=ud, fds=fds):
                        if ud >= 2:
                            gp.wait_ge(s_dd.h, ud - 1)
                        fds.inc(gp.dma_start(
                            out=wd_buf2[:, (ud % 2) * 1024:(ud % 2 + 1) * 1024],
                            in_=ffn_img(2, e8)[:, dmh * 1024:(dmh + 1) * 1024]), 16)
                    sec("gpsimd", g_ld_d)
                    fd_sems.append((fds, fds.n, ud % 2))

                def t_dn(t, g3=g3, dmh=dmh, fd_sems=fd_sems, hid_t=s_hid.n):
                    bank_guard(t)
                    t.wait_ge(s_hid.h, hid_t)
                    for e8 in range(64):
                        fds, tgt, slot = fd_sems[e8]
                        t.wait_ge(fds.h, tgt)
                        mm = None
                        for i in range(4):
                            for dr in range(2):
                                mm = t.matmul(
                                    pb[2 * i + dr][:, :],
                                    hidT[:, e8 * 512 + i * 128: e8 * 512 + (i + 1) * 128],
                                    wd_buf2[:, slot * 1024 + dr * 512:
                                            slot * 1024 + (dr + 1) * 512],
                                    start=(e8 == 0), stop=(e8 == 63))
                        s_dd.inc(mm)
                sec("tensor", t_dn)

                for i in range(4):
                    unit = (g3 * 2 + dmh) * 4 + i
                    yos = [s_yo, s_yo2][unit % 2]

                    def v_fevf(v, g3=g3, dmh=dmh, i=i, unit=unit, dd_t=s_dd.n,
                               yos=yos, yo_t=yos.n):
                        v.wait_ge(s_dd.h, dd_t)
                        if yo_t > 0:
                            v.wait_ge(yos.h, yo_t)
                        v.tensor_sub(out_sb[:, (unit % 2) * 1024:(unit % 2) * 1024 + 512],
                                     x2tile(i)[:, dmh * 1024: dmh * 1024 + 512],
                                     pb[2 * i][:, :])
                        s_fev.inc(v.tensor_sub(
                            out_sb[:, (unit % 2) * 1024 + 512:(unit % 2 + 1) * 1024],
                            x2tile(i)[:, dmh * 1024 + 512:(dmh + 1) * 1024],
                            pb[2 * i + 1][:, :]))
                    sec("vector", v_fevf)

                    def sy_y(s, g3=g3, dmh=dmh, i=i, unit=unit, yos=yos,
                             fev_t=s_fev.n):
                        s.wait_ge(s_fev.h, fev_t)
                        yos.inc(s.dma_start(
                            out=y_sh[(g3 * 4 + i) * 128:(g3 * 4 + i + 1) * 128,
                                     dmh * 1024:(dmh + 1) * 1024],
                            in_=out_sb[:, (unit % 2) * 1024:(unit % 2 + 1) * 1024]), 16)
                    sec("sync", sy_y)

        def sy_fin(s, a=s_yo.n, b=s_yo2.n):
            s.wait_ge(s_yo.h, a)
            s.wait_ge(s_yo2.h, b)
        sec("sync", sy_fin)

    return nc




_CACHE = {}


def kernel(L0_attn_x, L0_attn_Wq, L0_attn_Wk, L0_attn_Wv, L0_attn_Wg, L0_attn_Wp,
           L0_ffn_W_up, L0_ffn_W_gate, L0_ffn_W_down):
    if "nc" not in _CACHE:
        _CACHE["nc"] = build()
    nc = _CACHE["nc"]
    X = np.ascontiguousarray(np.asarray(L0_attn_x, dtype=np.float32).reshape(NCORES * TOK, D))
    Wq = np.asarray(L0_attn_Wq, dtype=np.float32)
    Wk = np.asarray(L0_attn_Wk, dtype=np.float32)
    Wv = np.asarray(L0_attn_Wv, dtype=np.float32)
    Wg = np.asarray(L0_attn_Wg, dtype=np.float32)
    Wp = np.asarray(L0_attn_Wp, dtype=np.float32)
    Wu = np.asarray(L0_ffn_W_up, dtype=np.float32)
    Wg2 = np.asarray(L0_ffn_W_gate, dtype=np.float32)
    Wd = np.asarray(L0_ffn_W_down, dtype=np.float32)
    in_maps = []
    for c in range(NCORES):
        in_maps.append({
            "x_sh": np.ascontiguousarray(X[c * TOK:(c + 1) * TOK]),
            "wk_sh": np.ascontiguousarray(Wk[c * 256:(c + 1) * 256, :]),
            "wq_sh": np.ascontiguousarray(Wq[c * 256:(c + 1) * 256, :]),
            "wv_sh": np.ascontiguousarray(Wv[:, c * 256:(c + 1) * 256]),
            "wg_sh": np.ascontiguousarray(Wg[:, c * 256:(c + 1) * 256]),
            "wp_sh": np.ascontiguousarray(Wp[:, c * 256:(c + 1) * 256]),
            "kvmask": (np.arange(8) < c).astype(np.int32).reshape(1, 8),
            "wu_sh": np.ascontiguousarray(Wu[c * 1024:(c + 1) * 1024, :]),
            "wg2_sh": np.ascontiguousarray(Wg2[c * 1024:(c + 1) * 1024, :]),
            "wd_sh": np.ascontiguousarray(Wd[:, c * 1024:(c + 1) * 1024]),
        })
    res = run_bass_kernel_spmd(nc, in_maps, core_ids=list(range(NCORES)))
    out = np.concatenate([res.results[c]["y_sh"] for c in range(NCORES)], axis=0)
    return out.reshape(1, NCORES * TOK, D)


# version 05 (final)
# speedup vs baseline: 1.1784x; 1.0205x over previous
import numpy as np
import ml_dtypes
import concourse.bass as bass
import concourse.mybir as mybir
from concourse.bass_utils import run_bass_kernel_spmd
from contextlib import ExitStack

F32 = mybir.dt.float32
BF16 = mybir.dt.bfloat16
AF = mybir.ActivationFunctionType
ALU = mybir.AluOpType

NCORES = 8
TOK = 2048
D = 2048
ND = 16
NG = 4
GT = 512
NT = 16
NMAT = 5
EPS_LN = 1e-5
EPS_RMS = float(np.finfo(np.float32).eps)


def build(debug_outputs=True):
    nc = bass.Bass("TRN2", target_bir_lowering=False)

    x_sh = nc.dram_tensor("x_sh", [TOK, D], F32, kind="ExternalInput").ap()
    w_sh = {}
    w_sh["wk"] = nc.dram_tensor("wk_sh", [256, D], F32, kind="ExternalInput").ap()
    w_sh["wv"] = nc.dram_tensor("wv_sh", [D, 256], F32, kind="ExternalInput").ap()
    w_sh["wq"] = nc.dram_tensor("wq_sh", [256, D], F32, kind="ExternalInput").ap()
    w_sh["wg"] = nc.dram_tensor("wg_sh", [D, 256], F32, kind="ExternalInput").ap()
    w_sh["wp"] = nc.dram_tensor("wp_sh", [D, 256], F32, kind="ExternalInput").ap()
    kvmask = nc.dram_tensor("kvmask", [1, 8], mybir.dt.int32, kind="ExternalInput").ap()
    wu_sh = nc.dram_tensor("wu_sh", [1024, D], F32, kind="ExternalInput").ap()
    wg2_sh = nc.dram_tensor("wg2_sh", [1024, D], F32, kind="ExternalInput").ap()
    wd_sh = nc.dram_tensor("wd_sh", [D, 1024], F32, kind="ExternalInput").ap()

    y_sh = nc.dram_tensor("y_sh", [TOK, D], F32, kind="ExternalOutput").ap()

    MAT = {"wk": 0, "wv": 1, "wq": 2, "wg": 3, "wp": 4}
    aw_sh1 = nc.dram_tensor("aw_sh1", [2 * 2 * 128, D], BF16).ap()
    aw_all1 = nc.dram_tensor("aw_all1", [NCORES * 2 * 2 * 128, D], BF16, addr_space="Shared").ap()
    aw_sh2 = nc.dram_tensor("aw_sh2", [3 * 2 * 128, D], BF16).ap()
    aw_all2 = nc.dram_tensor("aw_all2", [NCORES * 3 * 2 * 128, D], BF16, addr_space="Shared").ap()

    def aw_img(m, s):
        if m < 2:
            row = ((s // 2) * 4 + m * 2 + (s % 2)) * 128
            return aw_all1[row:row + 128, :]
        row = ((s // 2) * 6 + (m - 2) * 2 + (s % 2)) * 128
        return aw_all2[row:row + 128, :]

    ktj_dram = nc.dram_tensor("ktj_dram", [NT * 128, D], BF16).ap()
    kvacc = nc.dram_tensor("kvacc", [D, D], F32).ap()
    kv_in = nc.dram_tensor("kv_in", [D, D], BF16).ap()
    kv_all = nc.dram_tensor("kv_all", [NCORES * D, D], BF16, addr_space="Shared").ap()
    kvp_dram = nc.dram_tensor("kvp_dram", [D, D], BF16).ap()
    x2_dram = nc.dram_tensor("x2_dram", [TOK, D], F32).ap()
    I8 = mybir.dt.int8
    ffn_sh = nc.dram_tensor("ffn_sh", [24 * 128, D], I8).ap()
    ffn_all = nc.dram_tensor("ffn_all", [NCORES * 24 * 128, D], I8, addr_space="Shared").ap()
    ar_in = nc.dram_tensor("ar_in", [1, 4], F32).ap()
    ar_out = nc.dram_tensor("ar_out", [1, 4], F32).ap()

    def ffn_img(m, blk):
        row = ((blk // 8) * 24 + m * 8 + (blk % 8)) * 128
        return ffn_all[row:row + 128, :]

    with ExitStack() as stk:
        e = stk.enter_context

        def sb(name, shape, dt):
            return e(nc.sbuf_tensor(name, shape, dt))

        def psum(name, shape, dt):
            return e(nc.psum_tensor(name, shape, dt))

        x_sb = sb("x_sb", [128, 2 * D], F32)
        xn_sb = sb("xn_sb", [128, 2 * D], BF16)
        xnT_g = sb("xnT_g", [128, ND * GT], BF16)
        v_sb = sb("v_sb", [128, ND * D], BF16)
        ktj_sb = sb("ktj_sb", [128, 4 * D], BF16)
        qT_g = ktj_sb
        stats = sb("stats", [128, NT * 8], F32)
        bnbuf = sb("bnbuf", [128, 24], F32)
        maggr = sb("maggr", [128, 2 * NT], F32)
        cs_all = sb("cs_all", [128, NT], F32)
        m_all = sb("m_all", [128, NT], F32)
        wo_buf = sb("wo_buf", [128, 2 * D], BF16)
        wt_buf = sb("wt_buf", [128, 4 * 512], BF16)

        idn = sb("idn", [128, 128], BF16)
        masks = sb("masks", [128, 1024], BF16)
        g_sb = sb("g_sb", [128, 4 * D], BF16)
        sT_buf = xnT_g
        ogT_g = sb("ogT_g", [128, ND * GT], BF16)
        og_sb = sb("og_sb", [128, 1024], BF16)
        ktmp4 = og_sb
        kvp_buf = sb("kvp_buf", [128, 2 * 1024], BF16)
        ktjs_buf = sb("ktjs_buf", [128, 2 * D], BF16)
        wp_buf = sb("wp_buf", [128, 2 * 1024], BF16)
        att_sb = sb("att_sb", [128, 2 * 1024], F32)
        acc_g = att_sb
        kvt_buf = ogT_g
        nat_sb = wp_buf
        img_sb_t = ktjs_buf
        mk8i = sb("mk8i", [1, 8], mybir.dt.int32)
        tern_sb = sb("tern_sb", [128, D], BF16)
        fpc_sb = sb("fpc_sb", [128, D], F32)
        fimg_sb = sb("fimg_sb", [128, D], mybir.dt.int8)
        absum = sb("absum", [128, 4], F32)
        svals = sb("svals", [1, 16], F32)
        scbc = sb("scbc", [128, 16], F32)
        ones_col = sb("ones_col", [128, 1], F32)

        pb = [psum(f"pb{i}", [128, 512], F32) for i in range(8)]
        TPA, TPB = 6, 7
        PKA, PKB = 4, 5

        class Sem:
            def __init__(self, name):
                self.h = e(nc.semaphore(name))
                self.n = 0

            def inc(self, inst, k=1):
                inst.then_inc(self.h, k)
                self.n += k
                return self.n

        s_ini = Sem("s_ini")
        s_pl = Sem("s_pl")
        s_mk = Sem("s_mk")
        s_po = Sem("s_po")
        s_cc = Sem("s_cc")
        s_cA1 = Sem("s_cA1")
        s_cA2 = Sem("s_cA2")
        s_cAR = Sem("s_cAR")
        s_cKV = Sem("s_cKV")
        s_cFN = Sem("s_cFN")
        s_x = [Sem("s_xA"), Sem("s_xB")]
        tile_ctr = [0]
        sv = Sem("sv")
        sc_ = Sem("sc")
        s_xn = Sem("s_xn")
        s_tp = Sem("s_tp")
        s_tpe = Sem("s_tpe")
        s_te = Sem("s_te")
        s_wo = [Sem("s_woA"), Sem("s_woB")]
        s_kp = Sem("s_kp")
        s_kev = Sem("s_kev")
        s_wt = [Sem(f"s_wt{i}") for i in range(4)]
        s_vd = Sem("s_vd")
        s_vev = Sem("s_vev")
        s_kmm = Sem("s_kmm")
        s_kt = Sem("s_kt")
        s_kte = Sem("s_kte")
        s_kve = Sem("s_kve")
        s_kvo = Sem("s_kvo")
        s_ks = Sem("s_ks")
        s_sum = Sem("s_sum")
        s_pu = Sem("s_pu")
        s_sm = [Sem("s_smA"), Sem("s_smB")]
        s_sv2 = [Sem("s_sv2A"), Sem("s_sv2B")]
        s_pu2 = Sem("s_pu2")
        s_pvx = Sem("s_pvx")
        s_kvw2 = Sem("s_kvw2")
        sv2_marks = {}
        s_kvw = Sem("s_kvw")
        s_qe = Sem("s_qe")
        s_ge = Sem("s_ge")
        s_kj = [Sem("s_kjA"), Sem("s_kjB")]
        s_kjd = Sem("s_kjd")
        s_se = Sem("s_se")
        s_kvl = [Sem("s_kvlA"), Sem("s_kvlB")]
        s_kvd = Sem("s_kvd")
        s_og = Sem("s_og")
        s_ogt = Sem("s_ogt")
        s_oge = Sem("s_oge")
        s_wp = [Sem("s_wpA"), Sem("s_wpB")]
        s_wpd = Sem("s_wpd")
        s_aev = Sem("s_aev")
        s_acc = Sem("s_acc")
        s_x2i = Sem("s_x2i")
        s_fin = Sem("s_fin")
        s_scb = Sem("s_scb")
        s_ar2 = Sem("s_ar2")
        s_fb = Sem("s_fb")
        s_fl = Sem("s_fl")
        s_fl2 = Sem("s_fl2")
        s_fa = Sem("s_fa")
        s_ar = Sem("s_ar")
        s_tn = Sem("s_tn")
        s_fo = Sem("s_fo")
        s_fu = Sem("s_fu")
        s_fu2 = Sem("s_fu2")
        s_fg = Sem("s_fg")
        s_fg2 = Sem("s_fg2")
        s_fd = Sem("s_fd")
        s_fd2 = Sem("s_fd2")
        s_fud = Sem("s_fud")
        s_t1 = Sem("s_t1")
        s_hid = Sem("s_hid")
        s_dd = Sem("s_dd")
        s_fev = Sem("s_fev")
        s_yo = Sem("s_yo")
        s_x2l = [Sem(f"s_x2l{i}") for i in range(4)]

        block = e(nc.Block())

        def sec(eng, f):
            getattr(block, eng)(f)

        tp_units = [0]
        tile_marks = []

        def bank_guard(t):
            for s in (s_kev, s_se, s_vev, s_kve, s_oge, s_aev, s_tpe, s_og,
                      s_fev, s_hid, s_t1, s_dd, s_fud, s_scb):
                if s.n > 0:
                    t.wait_ge(s.h, s.n)

        def emit_transpose_unit(t, srcs, extra_wait=None):
            pg = tp_units[0]
            if pg >= 2:
                t.wait_ge(s_tpe.h, pg - 1)
            if extra_wait is not None:
                extra_wait(t)
            slot = pb[TPA + (pg % 2)][:, :].bitcast(BF16)
            last = None
            for idx, in_ap in enumerate(srcs):
                last = t.transpose(slot[:, idx * 128:(idx + 1) * 128], in_ap, idn[:, :])
            s_tp.inc(last)
            tp_units[0] += 1
            return pg

        def emit_evac_unit(v, pg, dsts):
            v.wait_ge(s_tp.h, pg + 1)
            slot = pb[TPA + (pg % 2)][:, :].bitcast(BF16)
            last = None
            for idx, out_ap in enumerate(dsts):
                last = v.tensor_copy(out_ap, slot[:, idx * 128:(idx + 1) * 128])
            s_tpe.inc(last)
            return last

        def _init(g):
            idn_f = x_sb[:, 0:128]
            g.iota(idn_f, pattern=[[-1, 128]], channel_multiplier=1,
                   allow_small_or_imprecise_dtypes=True)
            g.drain()
            g.tensor_single_scalar(idn[:, :], idn_f, 0.0, ALU.is_equal)
            g.drain()
            g.iota(x_sb[:, 0:1024], pattern=[[1, 1024]],
                   channel_multiplier=-1, base=-384,
                   allow_small_or_imprecise_dtypes=True)
            g.drain()
            last = g.tensor_single_scalar(masks[:, 0:1024], x_sb[:, 0:1024], 0.0, ALU.is_ge)
            g.memset(absum[:, :], 0.0)
            g.memset(ones_col[:, :], 1.0)
            s_ini.inc(last)

        sec("gpsimd", _init)

        def sy_x2i(s):
            s_x2i.inc(s.dma_start(out=x2_dram, in_=x_sh), 16)
        sec("sync", sy_x2i)

        def g_mk(g):
            s_mk.inc(g.dma_start(out=mk8i[0:1, :], in_=kvmask), 16)
        sec("gpsimd", g_mk)
        mk_pl_t = s_mk.n

        def emit_prep():
            for bi, (mname, sl) in enumerate(
                    [(m, s) for m in ["wk", "wv", "wq", "wg", "wp"] for s in range(2)]):
                m = MAT[mname]
                type_o = mname in ("wk", "wq")

                def g_load(g, mname=mname, sl=sl, bi=bi, type_o=type_o,
                           tp_before=tp_units[0]):
                    if bi > 0:
                        g.wait_ge(s_tp.h, tp_before)
                    if type_o:
                        s_pl.inc(g.dma_start(out=nat_sb[:, :],
                                             in_=w_sh[mname][sl * 128:(sl + 1) * 128, :]), 16)
                    else:
                        for r in range(16):
                            s_pl.inc(g.dma_start(
                                out=nat_sb[:, r * 128:(r + 1) * 128],
                                in_=w_sh[mname][r * 128:(r + 1) * 128, sl * 128:(sl + 1) * 128]), 16)
                sec("gpsimd", g_load)
                pl_target = s_pl.n

                pgs = []

                def t_tr(t, pl_target=pl_target, pgs=pgs):
                    def w0(tt):
                        tt.wait_ge(s_ini.h, 1)
                        tt.wait_ge(s_pl.h, pl_target)
                    for pr in range(8):
                        pgs.append(emit_transpose_unit(
                            t,
                            [nat_sb[:, (2 * pr) * 128:(2 * pr + 1) * 128],
                             nat_sb[:, (2 * pr + 1) * 128:(2 * pr + 2) * 128]],
                            extra_wait=w0 if pr == 0 else None))
                sec("tensor", t_tr)

                def v_ev(v, pgs=pgs, po_before=s_po.n, bi=bi):
                    if bi > 0:
                        v.wait_ge(s_po.h, po_before)
                    for pr in range(8):
                        emit_evac_unit(v, pgs[pr], [
                            img_sb_t[:, (2 * pr) * 128:(2 * pr + 1) * 128],
                            img_sb_t[:, (2 * pr + 1) * 128:(2 * pr + 2) * 128]])
                sec("vector", v_ev)

                def g_out(g, m=m, sl=sl, pe_target=s_tpe.n):
                    g.wait_ge(s_tpe.h, pe_target)
                    if m < 2:
                        dst = aw_sh1[(m * 2 + sl) * 128:(m * 2 + sl + 1) * 128, :]
                    else:
                        dst = aw_sh2[((m - 2) * 2 + sl) * 128:((m - 2) * 2 + sl + 1) * 128, :]
                    s_po.inc(g.dma_start(out=dst, in_=img_sb_t[:, 0:D]), 16)
                sec("gpsimd", g_out)

                if m == 1 and sl == 1:
                    def g_ag1(g, po_target=s_po.n):
                        g.wait_ge(s_po.h, po_target)
                        s_cA1.inc(g.collective_compute(
                            "AllGather", ALU.bypass, replica_groups=[list(range(NCORES))],
                            ins=[aw_sh1], outs=[aw_all1]))
                    sec("gpsimd", g_ag1)

            def g_ag2(g, po_target=s_po.n):
                g.wait_ge(s_po.h, po_target)
                s_cA2.inc(g.collective_compute(
                    "AllGather", ALU.bypass, replica_groups=[list(range(NCORES))],
                    ins=[aw_sh2], outs=[aw_all2]))
            sec("gpsimd", g_ag2)

        emit_prep()

        slab_f = g_sb[:, :].bitcast(F32)

        def slab_load(g, k, slot):
            if k < 8:
                g.dma_start(out=slab_f[:, slot * D:(slot + 1) * D],
                            in_=wu_sh[k * 128:(k + 1) * 128, :]).then_inc(
                                (s_fl if slot == 0 else s_fl2).h, 16)
            elif k < 16:
                r = k - 8
                g.dma_start(out=slab_f[:, slot * D:(slot + 1) * D],
                            in_=wg2_sh[r * 128:(r + 1) * 128, :]).then_inc(
                                (s_fl if slot == 0 else s_fl2).h, 16)
            else:
                s = k - 16
                for r in range(16):
                    g.dma_start(out=slab_f[:, slot * D + r * 128: slot * D + (r + 1) * 128],
                                in_=wd_sh[r * 128:(r + 1) * 128, s * 128:(s + 1) * 128]
                                ).then_inc((s_fl if slot == 0 else s_fl2).h, 16)
            (s_fl if slot == 0 else s_fl2).n += 16 * (1 if k < 16 else 16)

        def g_fpA_chunk(g, ks):
            for k in ks:
                if k >= 2:
                    g.wait_ge(s_fa.h, k - 1)
                slab_load(g, k, k % 2)
        fpA_marks = []
        _fl = [s_fl.n, s_fl2.n]
        _t0, _t1 = 0, 0
        for k in range(24):
            n = 16 * (1 if k < 16 else 16)
            if k % 2 == 0:
                _t0 += n
                fpA_marks.append((0, _t0))
            else:
                _t1 += n
                fpA_marks.append((1, _t1))

        def v_fpA_chunk(v, ks):
            if ks[0] == 0:
                v.wait_ge(s_ini.h, 1)
            for k in ks:
                slot, target = fpA_marks[k]
                v.wait_ge((s_fl if slot == 0 else s_fl2).h, target)
                m = 0 if k < 8 else (1 if k < 16 else 2)
                v.tensor_reduce(absum[:, 3:4], slab_f[:, slot * D:(slot + 1) * D],
                                mybir.AxisListType.X, ALU.add, apply_absolute_value=True)
                v.drain()
                s_fa.inc(v.tensor_add(absum[:, m:m + 1], absum[:, m:m + 1], absum[:, 3:4]))
                v.drain()

        def emit_xn_phase(g, with_stats):
            g4 = g * 4
            for i in range(4):
                ti = g4 + i
                n_tile = tile_ctr[0]
                tile_ctr[0] += 1
                xslot = n_tile % 2
                sxs = s_x[xslot]

                def sy_x(s, ti=ti, xslot=xslot, sxs=sxs, n_tile=n_tile):
                    if n_tile == 0:
                        s.wait_ge(s_ini.h, 1)
                    if s_xn.n >= 2:
                        s.wait_ge(s_xn.h, s_xn.n - 1)
                    sxs.inc(s.dma_start(out=x_sb[:, xslot * D:(xslot + 1) * D],
                                        in_=x_sh[ti * 128:(ti + 1) * 128, :]), 16)
                sec("sync", sy_x)
                x_sem = sxs
                x_target = sxs.n

                base = ti * 8
                col = lambda k, base=base: stats[:, base + k:base + k + 1]
                w1_c, sq1_c, r1_c, t_c, t2_c, w2_c, sq2_c, r2_c = [col(k) for k in range(8)]
                m_c = maggr[:, 2 * ti:2 * ti + 1]
                v_c = maggr[:, 2 * ti + 1:2 * ti + 2]

                if with_stats:
                    def v_stats(v, ti=ti, xslot=xslot, x_target=x_target,
                                w1_c=w1_c, sq1_c=sq1_c, r1_c=r1_c, t_c=t_c,
                                t2_c=t2_c, w2_c=w2_c, sq2_c=sq2_c, r2_c=r2_c,
                                m_c=m_c, v_c=v_c, n_tile=n_tile, x_sem=x_sem):
                        v.wait_ge(x_sem.h, x_target)
                        for cq in range(4):
                            v.bn_stats(bnbuf[:, cq * 6:(cq + 1) * 6],
                                       x_sb[:, xslot * D + cq * 512: xslot * D + (cq + 1) * 512])
                        v.drain()
                        v.bn_aggr(maggr[:, 2 * ti:2 * ti + 2], bnbuf[:, 0:24])
                        v.drain()
                        sv.inc(v.tensor_scalar_add(w1_c, v_c, EPS_LN))
                        v.wait_ge(sc_.h, sc_.n + 1)
                        v.reciprocal(r1_c, sq1_c)
                        v.drain()
                        v.tensor_mul(t_c, v_c, r1_c)
                        v.drain()
                        v.tensor_mul(t2_c, t_c, r1_c)
                        v.drain()
                        sv.inc(v.tensor_scalar_add(w2_c, t2_c, EPS_RMS))
                        v.wait_ge(sc_.h, sc_.n + 2)
                        v.reciprocal(r2_c, sq2_c)
                        v.drain()
                        v.tensor_mul(cs_all[:, ti:ti + 1], r1_c, r2_c)
                        v.tensor_copy(m_all[:, ti:ti + 1], maggr[:, 2 * ti:2 * ti + 1])
                        v.drain()
                        if n_tile >= 2:
                            v.wait_ge(s_tpe.h, tile_marks[n_tile - 2])
                        s_xn.inc(v.tensor_scalar(
                            xn_sb[:, (ti % 2) * D:(ti % 2 + 1) * D],
                            x_sb[:, xslot * D:(xslot + 1) * D],
                            m_c, cs_all[:, ti:ti + 1], ALU.subtract, ALU.mult))
                    sec("vector", v_stats)

                    def sc_chain(s, w1_c=w1_c, sq1_c=sq1_c, w2_c=w2_c, sq2_c=sq2_c,
                                 sv_base=sv.n):
                        s.wait_ge(sv.h, sv_base - 1)
                        sc_.inc(s.activation(sq1_c, w1_c, AF.Sqrt))
                        s.wait_ge(sv.h, sv_base)
                        sc_.inc(s.activation(sq2_c, w2_c, AF.Sqrt))
                    sec("scalar", sc_chain)
                else:
                    def v_norm(v, ti=ti, xslot=xslot, x_target=x_target, n_tile=n_tile,
                               x_sem=x_sem):
                        v.wait_ge(x_sem.h, x_target)
                        if n_tile >= 2:
                            v.wait_ge(s_tpe.h, tile_marks[n_tile - 2])
                        s_xn.inc(v.tensor_scalar(
                            xn_sb[:, (ti % 2) * D:(ti % 2 + 1) * D],
                            x_sb[:, xslot * D:(xslot + 1) * D],
                            m_all[:, ti:ti + 1], cs_all[:, ti:ti + 1],
                            ALU.subtract, ALU.mult))
                    sec("vector", v_norm)

                xn_target = s_xn.n
                pgs_x = []

                def t_xtr(t, ti=ti, pgs_x=pgs_x, xn_target=xn_target):
                    def w0(tt):
                        tt.wait_ge(s_xn.h, xn_target)
                    for pr in range(8):
                        pgs_x.append(emit_transpose_unit(
                            t,
                            [xn_sb[:, (ti % 2) * D + (2 * pr) * 128: (ti % 2) * D + (2 * pr + 1) * 128],
                             xn_sb[:, (ti % 2) * D + (2 * pr + 1) * 128: (ti % 2) * D + (2 * pr + 2) * 128]],
                            extra_wait=w0 if pr == 0 else None))
                sec("tensor", t_xtr)

                def v_xev(v, i=i, pgs_x=pgs_x):
                    for pr in range(8):
                        d0, d1 = 2 * pr, 2 * pr + 1
                        emit_evac_unit(v, pgs_x[pr], [
                            xnT_g[:, d0 * GT + i * 128: d0 * GT + (i + 1) * 128],
                            xnT_g[:, d1 * GT + i * 128: d1 * GT + (i + 1) * 128]])
                sec("vector", v_xev)
                tile_marks.append(s_tpe.n)

        def emit_typeO_proj(g, mname, act, evac_fn):
            m = MAT[mname]
            for o in range(16):
                u_kp = s_kp.n

                wos = s_wo[u_kp % 2]

                def g_w(gp, o=o, m=m, u_kp=u_kp, wos=wos):
                    if o == 0:
                        gp.wait_ge((s_cA1 if m < 2 else s_cA2).h, 1)
                    if u_kp >= 2:
                        gp.wait_ge(s_kp.h, u_kp - 1)
                    wos.inc(gp.dma_start(
                        out=wo_buf[:, (u_kp % 2) * D:(u_kp % 2 + 1) * D],
                        in_=aw_img(m, o)), 16)
                sec("sync", g_w)

                def t_kp(t, o=o, u=u_kp, wos=wos, wo_target=wos.n,
                         first=(o == 0), tpe_t=s_tp.n):
                    if first:
                        bank_guard(t)
                        t.wait_ge(s_tpe.h, tpe_t)
                    t.wait_ge(wos.h, wo_target)
                    if u >= 2:
                        t.wait_ge(s_kev.h, u - 1)
                    pps = pb[PKA + (u % 2)]
                    mm = None
                    for d in range(ND):
                        mm = t.matmul(pps[:, :],
                                      wo_buf[:, (u % 2) * D + d * 128:(u % 2) * D + (d + 1) * 128],
                                      xnT_g[:, d * GT:(d + 1) * GT],
                                      start=(d == 0), stop=(d == ND - 1))
                    s_kp.inc(mm)
                sec("tensor", t_kp)

                def s_ev(s, o=o, u=u_kp):
                    s.wait_ge(s_kp.h, u + 1)
                    evac_fn(s, o, u, pb[PKA + (u % 2)])
                sec("scalar", s_ev)

        def emit_typeT_proj(g, mname, act, out_fn):
            m = MAT[mname]
            for eq in range(4):
                u_vp = (s_vd.n // 16)

                for d in range(ND):
                    w_wt = sum(s.n for s in s_wt) // 16
                    wts = s_wt[w_wt % 4]

                    def g_w(gp, m=m, eq=eq, d=d, w=w_wt, wts=wts,
                            first=(eq == 0 and d == 0)):
                        if first:
                            gp.wait_ge((s_cA1 if m < 2 else s_cA2).h, 1)
                        if w >= 4:
                            gp.wait_ge(s_vd.h, w - 3)
                        wts.inc(gp.dma_start(
                            out=wt_buf[:, (w % 4) * 512:(w % 4 + 1) * 512],
                            in_=aw_img(m, d)[:, eq * 512:(eq + 1) * 512]), 16)
                    sec("sync", g_w)

                    def t_vp(t, eq=eq, d=d, w=w_wt, u_vp=u_vp, wts=wts, wt_target=wts.n,
                             vev_before=s_vev.n, first=(eq == 0 and d == 0), tpe_t=s_tp.n):
                        if first:
                            bank_guard(t)
                            t.wait_ge(s_tpe.h, tpe_t)
                        t.wait_ge(wts.h, wt_target)
                        if d == 0 and u_vp >= 1:
                            t.wait_ge(s_vev.h, vev_before)
                        mm = None
                        for i in range(4):
                            mm = t.matmul(pb[i][:, :],
                                          xnT_g[:, d * GT + i * 128: d * GT + (i + 1) * 128],
                                          wt_buf[:, (w % 4) * 512:(w % 4 + 1) * 512],
                                          start=(d == 0), stop=(d == ND - 1))
                        s_vd.inc(mm)
                    sec("tensor", t_vp)

                def s_ev(s, eq=eq, act=act, out_fn=out_fn, vd_t=s_vd.n):
                    s.wait_ge(s_vd.h, vd_t)
                    for i in range(4):
                        s_vev.inc(s.activation(out_fn(i, eq), pb[i][:, :], act))
                sec("scalar", s_ev)

        for g in range(NG):
            sec("gpsimd", lambda gp, ks=list(range(g * 6, g * 6 + 6)): g_fpA_chunk(gp, ks))
            emit_xn_phase(g, with_stats=True)
            sec("vector", lambda v, ks=list(range(g * 6, g * 6 + 6)): v_fpA_chunk(v, ks))

            ks_before = s_ks.n

            def k_evac(s, o, u, pps, g=g, ks_before=ks_before):
                if o == 0 and ks_before > 0:
                    s.wait_ge(s_ks.h, ks_before)
                last = None
                for i in range(4):
                    last = s.activation(ktj_sb[:, i * D + o * 128: i * D + (o + 1) * 128],
                                        pps[:, i * 128:(i + 1) * 128], AF.Silu)
                s_kev.inc(last)
            emit_typeO_proj(g, "wk", AF.Silu, k_evac)

            emit_typeT_proj(g, "wv", AF.Silu,
                            lambda i, eq, g=g: v_sb[:, (g * 4 + i) * D + eq * 512:
                                                    (g * 4 + i) * D + (eq + 1) * 512])

            kev_t = s_kev.n
            vev_t = s_vev.n

            def kv_transpose(t, u):
                if u >= 4:
                    t.wait_ge(s_kte.h, u - 3)
                g_, r_ = u // 64, u % 64
                o_, j_ = r_ // 4, r_ % 4
                s_kt.inc(t.transpose(
                    pb[4 + u % 4][:, :].bitcast(BF16)[:, 0:128],
                    ktj_sb[:, j_ * D + o_ * 128: j_ * D + (o_ + 1) * 128],
                    idn[:, :]))

            for o in range(16):
                for j in range(4):
                    kj = g * 64 + o * 4 + j

                    def t_kv1(t, g=g, o=o, j=j, kj=kj, kev_t=kev_t, vev_t=vev_t):
                        if o == 0 and j == 0:
                            bank_guard(t)
                            t.wait_ge(s_kev.h, kev_t)
                            t.wait_ge(s_vev.h, vev_t)
                            for la in range(4):
                                kv_transpose(t, kj + la)
                        if o >= 1 and j == 0:
                            t.wait_ge(s_kve.h, g * 16 + o)
                        if (kj + 4) % 64 != 0 or (kj + 4) // 64 == kj // 64:
                            if (kj + 4) < (g + 1) * 64:
                                kv_transpose(t, kj + 4)
                        t.wait_ge(s_kte.h, kj + 1)
                        mm = None
                        for er in range(4):
                            mm = t.matmul(pb[er][:, :],
                                          ktmp4[:, (kj % 4) * 128:(kj % 4 + 1) * 128],
                                          v_sb[:, (g * 4 + j) * D + er * 512:
                                               (g * 4 + j) * D + (er + 1) * 512],
                                          start=(j == 0), stop=(j == 3))
                        s_kmm.inc(mm)
                    sec("tensor", t_kv1)

                    def v_kte(v, kj=kj):
                        v.wait_ge(s_kt.h, kj + 1)
                        if kj >= 4:
                            v.wait_ge(s_kmm.h, kj - 3)
                        s_kte.inc(v.tensor_copy(
                            ktmp4[:, (kj % 4) * 128:(kj % 4 + 1) * 128],
                            pb[4 + kj % 4][:, :].bitcast(BF16)[:, 0:128]))
                    sec("vector", v_kte)

                def v_kve(v, g=g, o=o, kvo_before=s_kvo.n):
                    v.wait_ge(s_kmm.h, (g * 16 + o + 1) * 4)
                    if kvo_before >= 16:
                        v.wait_ge(s_kvo.h, kvo_before)
                    last = None
                    for er in range(4):
                        last = v.tensor_copy(fpc_sb[:, er * 512:(er + 1) * 512], pb[er][:, :])
                    s_kve.inc(last)
                sec("vector", v_kve)

                def sy_kv(s, g=g, o=o):
                    s.wait_ge(s_kve.h, g * 16 + o + 1)
                    if g == 0:
                        s_kvo.inc(s.dma_start(out=kvacc[o * 128:(o + 1) * 128, :],
                                              in_=fpc_sb[:, :]), 16)
                    else:
                        s_kvo.inc(s.dma_start(out=kvacc[o * 128:(o + 1) * 128, :],
                                              in_=fpc_sb[:, :], accum_op=ALU.add), 16)
                    s.wait_ge(s_kvo.h, s_kvo.n)
                sec("gpsimd", sy_kv)

            def sy_ks(s, g=g, kmm_t=s_kmm.n):
                s.wait_ge(s_kmm.h, kmm_t)
                for j in range(4):
                    s_ks.inc(s.dma_start(
                        out=ktj_dram[(g * 4 + j) * 128:(g * 4 + j + 1) * 128, :],
                        in_=ktj_sb[:, j * D:(j + 1) * D]), 16)
            sec("sync", sy_ks)


        def t_fpB1(t, fa_t=s_fa.n, ini_t=1):
            bank_guard(t)
            t.wait_ge(s_ini.h, ini_t)
            t.wait_ge(s_fa.h, fa_t)
            s_fb.inc(t.matmul(pb[PKA][0:4, 0:1], absum[:, 0:4], ones_col[:, :],
                              start=True, stop=True))
        sec("tensor", t_fpB1)
        fpB1_kp = s_fb.n

        def v_fpB1(v, kp_t=fpB1_kp):
            v.wait_ge(s_fb.h, kp_t)
            s_fa.inc(v.tensor_copy(absum[0:4, 3:4], pb[PKA][0:4, 0:1]))
        sec("vector", v_fpB1)
        fpB1_fa = s_fa.n

        onesf = og_sb[:, :].bitcast(F32)[0:1, 0:128]

        def g_fpB(g, fa_t=fpB1_fa, kvo_t=s_kvo.n):
            g.wait_ge(s_fa.h, fa_t)
            g.wait_ge(s_kvo.h, kvo_t)
            g.memset(onesf, 1.0)
            g.drain()
            g.dma_start(out=ar_in, in_=absum[0:4, 3:4]).then_inc(s_ar.h, 16)
            s_ar.n += 16
            g.wait_ge(s_ar.h, s_ar.n)
            s_cAR.inc(g.collective_compute(
                "AllReduce", ALU.add, replica_groups=[list(range(NCORES))],
                ins=[ar_in], outs=[ar_out]))
            g.wait_ge(s_cAR.h, 1)
            g.dma_start(out=svals[0:1, 0:4], in_=ar_out).then_inc(s_ar.h, 16)
            s_ar.n += 16
        sec("gpsimd", g_fpB)
        ar_t = s_ar.n

        def v_fpB2(v, ar_t=ar_t):
            v.wait_ge(s_ar.h, ar_t)
            v.tensor_scalar(svals[0:1, 4:7], svals[0:1, 0:3], 2.0 ** -24, 1e-8,
                            ALU.mult, ALU.add)
            v.drain()
            v.tensor_mul(svals[0:1, 7:8], svals[0:1, 5:6], svals[0:1, 6:7])
            v.tensor_scalar_mul(svals[0:1, 8:11], svals[0:1, 4:7], 0.5)
            v.drain()
            v.tensor_scalar_mul(svals[0:1, 11:14], svals[0:1, 8:11], -1.0)
            v.tensor_scalar_mul(svals[0:1, 14:15], svals[0:1, 4:5], -1.0)
            v.tensor_scalar_mul(svals[0:1, 15:16], svals[0:1, 7:8], -1.0)
            s_ar2.inc(v.drain())
        sec("vector", v_fpB2)

        def t_fpB2(t, ar2_t=s_ar2.n):
            bank_guard(t)
            t.wait_ge(s_ar2.h, ar2_t)
            s_fb.inc(t.matmul(pb[PKB][:, 0:16], onesf[0:1, :], svals[0:1, 0:16],
                              start=True, stop=True))
        sec("tensor", t_fpB2)
        fpB2_kp = s_fb.n

        def v_fpB3(v, kp_t=fpB2_kp):
            v.wait_ge(s_fb.h, kp_t)
            s_scb.inc(v.tensor_copy(scbc[:, :], pb[PKB][:, 0:16]))
        sec("vector", v_fpB3)

        def g_kvsum(g, kvo_t=s_kvo.n, mk_pl_t=mk_pl_t):
            kvt_f = ogT_g[:, :].bitcast(F32)
            g.wait_ge(s_kvo.h, kvo_t)
            g.wait_ge(s_mk.h, mk_pl_t)
            g.dma_start(out=kv_in, in_=kvacc).then_inc(s_sum.h, 16)
            s_sum.n += 16
            g.wait_ge(s_sum.h, s_sum.n)
            s_cKV.inc(g.collective_compute(
                "AllGather", ALU.bypass, replica_groups=[list(range(NCORES))],
                ins=[kv_in], outs=[kv_all]))
            g.wait_ge(s_cKV.h, 1)
            with g.register("mreg") as mreg:
                for ds in range(8):
                    g.memset(acc_g[:, :], 0.0)
                    g.drain()
                    base_units = s_pu.n

                    def load(jj):
                        sm = s_sm[jj % 2]
                        if jj >= 2:
                            g.wait_ge(s_pu.h, base_units + jj - 1)
                        g.dma_start(out=kvt_f[:, (jj % 2) * D:(jj % 2 + 1) * D],
                                    in_=kv_all[(jj * 16 + ds) * 128:
                                               (jj * 16 + ds + 1) * 128, :]
                                    ).then_inc(sm.h, 16)
                        sm.n += 16

                    load(0)
                    for jj in range(8):
                        if jj + 1 < 8:
                            load(jj + 1)
                        sm = s_sm[jj % 2]
                        g.wait_ge(sm.h, sm.n)
                        g.reg_load(mreg, mk8i[0:1, jj:jj + 1])
                        with g.If_eq(mreg, 1):
                            g.tensor_add(acc_g[:, :], acc_g[:, :],
                                         kvt_f[:, (jj % 2) * D:(jj % 2 + 1) * D]
                                         ).then_inc(s_pu.h, 1)
                        with g.Else():
                            g.sem_inc(s_pu.h, 1)
                        s_pu.n += 1
                    g.drain()
                    g.tensor_copy(kvt_f[:, 0:D].bitcast(BF16)[:, 0:D], acc_g[:, :])
                    g.drain()
                    g.dma_start(out=kvp_dram[ds * 128:(ds + 1) * 128, :],
                                in_=kvt_f[:, 0:D].bitcast(BF16)[:, 0:D]).then_inc(s_kvw.h, 16)
                    s_kvw.n += 16
                    g.wait_ge(s_kvw.h, s_kvw.n)
        sec("gpsimd", g_kvsum)

        s_fc = Sem("s_fc")
        s_tn2 = Sem("s_tn2")
        fc_marks = {}

        def g_fpC_load(gp, k):
            if k == 0:
                gp.wait_ge(s_pvx.h, 8)
            if k >= 1:
                gp.wait_ge(s_tn.h, k)
            if k < 8:
                s_fc.inc(gp.dma_start(out=fpc_sb[:, :],
                                      in_=wu_sh[k * 128:(k + 1) * 128, :]), 16)
            elif k < 16:
                r = k - 8
                s_fc.inc(gp.dma_start(out=fpc_sb[:, :],
                                      in_=wg2_sh[r * 128:(r + 1) * 128, :]), 16)
            else:
                s = k - 16
                for r in range(16):
                    s_fc.inc(gp.dma_start(
                        out=fpc_sb[:, r * 128:(r + 1) * 128],
                        in_=wd_sh[r * 128:(r + 1) * 128, s * 128:(s + 1) * 128]), 16)
            fc_marks[k] = s_fc.n

        def v_fpC_tern(v, k, scb_t):
            v.wait_ge(s_fc.h, fc_marks[k])
            v.wait_ge(s_scb.h, scb_t)
            m = 0 if k < 8 else (1 if k < 16 else 2)
            v.tensor_single_scalar(tern_sb[:, :], fpc_sb[:, :],
                                   scbc[:, 8 + m:9 + m], ALU.is_gt)
            v.drain()
            s_tn.inc(v.scalar_tensor_tensor(tern_sb[:, :], fpc_sb[:, :],
                                            scbc[:, 11 + m:12 + m], tern_sb[:, :],
                                            ALU.is_lt, ALU.subtract))

        def emit_fpC_slab(k, scb_t):
            sec("gpsimd", lambda gp, k=k: g_fpC_load(gp, k))
            sec("vector", lambda v, k=k, scb_t=scb_t: v_fpC_tern(v, k, scb_t))
            pgs_f = []

            def t_ftr(t, k=k, pgs_f=pgs_f, tn_t=s_tn.n):
                def w0(tt):
                    tt.wait_ge(s_tn.h, tn_t)
                for pr in range(8):
                    pgs_f.append(emit_transpose_unit(
                        t,
                        [tern_sb[:, (2 * pr) * 128:(2 * pr + 1) * 128],
                         tern_sb[:, (2 * pr + 1) * 128:(2 * pr + 2) * 128]],
                        extra_wait=w0 if pr == 0 else None))
            sec("tensor", t_ftr)

            def v_fev2(v, k=k, pgs_f=pgs_f, fo_t=16 * k):
                if fo_t > 0:
                    v.wait_ge(s_fo.h, fo_t)
                for pr in range(8):
                    emit_evac_unit(v, pgs_f[pr], [
                        fimg_sb[:, (2 * pr) * 128:(2 * pr + 1) * 128],
                        fimg_sb[:, (2 * pr + 1) * 128:(2 * pr + 2) * 128]])
            sec("vector", v_fev2)

            def g_fpC_out(gp, k=k, tpe_t=s_tpe.n):
                gp.wait_ge(s_tpe.h, tpe_t)
                s_fo.inc(gp.dma_start(out=ffn_sh[k * 128:(k + 1) * 128, :],
                                      in_=fimg_sb[:, :]), 16)
            sec("gpsimd", g_fpC_out)

        for g in range(NG):
            emit_xn_phase(g, with_stats=False)

            def q_evac(s, o, u, pps, ks_t=s_ks.n):
                if o == 0:
                    s.wait_ge(s_ks.h, ks_t)
                s_kev.inc(s.activation(qT_g[:, o * GT:(o + 1) * GT], pps[:, :], AF.Silu))
            emit_typeO_proj(g, "wq", AF.Silu, q_evac)

            emit_typeT_proj(g, "wg", AF.Sigmoid,
                            lambda i, eq: g_sb[:, i * D + eq * 512: i * D + (eq + 1) * 512])

            njt = 4 * (g + 1)
            qe_t = s_kev.n
            for jt in range(njt):
                u_kj = (s_kj[0].n + s_kj[1].n) // 16
                kjs = s_kj[u_kj % 2]

                def sy_kj(s, jt=jt, u=u_kj, po_t=s_po.n, kjs=kjs):
                    if u == 0:
                        s.wait_ge(s_po.h, po_t)
                    if u >= 2:
                        s.wait_ge(s_kjd.h, u - 1)
                    kjs.inc(s.dma_start(out=ktjs_buf[:, (u % 2) * D:(u % 2 + 1) * D],
                                        in_=ktj_dram[jt * 128:(jt + 1) * 128, :]), 16)
                sec("sync", sy_kj)

                def t_sT(t, g=g, jt=jt, u=u_kj, kjs=kjs, kj_target=kjs.n, qe_t=qe_t,
                         u_sT=s_se.n, ks_t=s_ks.n):
                    if jt == 0:
                        bank_guard(t)
                        t.wait_ge(s_kev.h, qe_t)
                        t.wait_ge(s_ks.h, ks_t)
                    t.wait_ge(kjs.h, kj_target)
                    if u_sT >= 2:
                        t.wait_ge(s_se.h, u_sT - 1)
                    pps = pb[PKA + (u_sT % 2)]
                    mm = None
                    for ft in range(ND):
                        mm = t.matmul(pps[:, :],
                                      ktjs_buf[:, (u % 2) * D + ft * 128:(u % 2) * D + (ft + 1) * 128],
                                      qT_g[:, ft * GT:(ft + 1) * GT],
                                      start=(ft == 0), stop=(ft == ND - 1))
                    s_kjd.inc(mm)
                sec("tensor", t_sT)

                def v_se(v, g=g, jt=jt, u_sT=s_se.n, kjd_t=s_kjd.n):
                    v.wait_ge(s_kjd.h, kjd_t)
                    pps = pb[PKA + (u_sT % 2)]
                    r = jt - 4 * g
                    if r >= 0:
                        s_se.inc(v.tensor_mul(sT_buf[:, jt * GT:(jt + 1) * GT],
                                              pps[:, :],
                                              masks[:, 384 - 128 * r: 896 - 128 * r]))
                    else:
                        s_se.inc(v.tensor_copy(sT_buf[:, jt * GT:(jt + 1) * GT], pps[:, :]))
                sec("vector", v_se)

            if g == 0:
                def sy_vload(s):
                    s.wait_ge(s_cKV.h, 1)
                    for ds in range(8, 16):
                        for jj in range(8):
                            sm = s_sv2[jj % 2]
                            u2 = (ds - 8) * 8 + jj
                            if u2 >= 2:
                                s.wait_ge(s_pu2.h, u2 - 1)
                            dst = kvp_buf[:, 0:2048] if jj % 2 == 0 else tern_sb[:, 0:2048]
                            sm.inc(s.dma_start(
                                out=dst,
                                in_=kv_all[(jj * 16 + ds) * 128:
                                           (jj * 16 + ds + 1) * 128, :]), 16)
                            sv2_marks[(ds, jj)] = sm.n
                sec("sync", sy_vload)

                def v_pref(v, marks=None, kvw2_0=s_kvw2.n):
                    with v.register("mr2") as mr2:
                        for ii, ds in enumerate(range(8, 16)):
                            v.memset(fpc_sb[:, :], 0.0)
                            v.drain()
                            for jj in range(8):
                                slot = kvp_buf[:, 0:2048] if jj % 2 == 0 else tern_sb[:, 0:2048]
                                v.wait_ge(s_sv2[jj % 2].h, sv2_marks[(ds, jj)])
                                v.reg_load(mr2, mk8i[0:1, jj:jj + 1])
                                with v.If_eq(mr2, 1):
                                    v.tensor_add(fpc_sb[:, :], fpc_sb[:, :], slot
                                                 ).then_inc(s_pu2.h, 1)
                                with v.Else():
                                    v.drain()
                                    v.sem_inc(s_pu2.h, 1)
                                s_pu2.n += 1
                                v.drain()
                            v.drain()
                            if ii >= 1:
                                v.wait_ge(s_kvw2.h, kvw2_0 + 16 * ii)
                            v.tensor_copy(wp_buf[:, 0:2048], fpc_sb[:, :])
                            s_pvx.inc(v.drain())
                sec("vector", v_pref)

                def g_pvo(gp, pvx_0=s_pvx.n - 8):
                    for ii, ds in enumerate(range(8, 16)):
                        gp.wait_ge(s_pvx.h, pvx_0 + ii + 1)
                        s_kvw2.inc(gp.dma_start(
                            out=kvp_dram[ds * 128:(ds + 1) * 128, :],
                            in_=wp_buf[:, 0:2048]), 16)
                sec("gpsimd", g_pvo)

            if g <= 1:
                for k in range(g * 12, g * 12 + 12):
                    emit_fpC_slab(k, 1)
            if g == 1:
                def g_fAG(gp, fo_t=s_fo.n):
                    gp.wait_ge(s_fo.h, fo_t)
                    s_cFN.inc(gp.collective_compute(
                        "AllGather", ALU.bypass, replica_groups=[list(range(NCORES))],
                        ins=[ffn_sh], outs=[ffn_all]))
                sec("gpsimd", g_fAG)

            se_t = s_se.n
            ge_t = s_vev.n
            for half in range(2):
                kv_sems = []
                for d in range(ND):
                    u_kv = (s_kvl[0].n + s_kvl[1].n) // 16
                    kvs = s_kvl[u_kv % 2]

                    def sy_kvl(s, g=g, half=half, d=d, u=u_kv, kvw_t=s_kvw.n,
                               kvw2_t=s_kvw2.n, kvs=kvs):
                        if u == 0:
                            s.wait_ge(s_kvw.h, kvw_t)
                            s.wait_ge(s_kvw2.h, kvw2_t)
                        if u >= 2:
                            s.wait_ge(s_kvd.h, u - 1)
                        kvs.inc(s.dma_start(
                            out=kvp_buf[:, (u % 2) * 1024:(u % 2 + 1) * 1024],
                            in_=kvp_dram[d * 128:(d + 1) * 128, half * 1024:(half + 1) * 1024]), 16)
                    sec("sync", sy_kvl)
                    kv_sems.append((kvs, kvs.n))

                def t_o(t, g=g, half=half, njt=njt, se_t=se_t, ge_t=ge_t,
                        kv_sems=kv_sems):
                    bank_guard(t)
                    t.wait_ge(s_se.h, se_t)
                    t.wait_ge(s_vev.h, ge_t)
                    for i in range(4):
                        first = True
                        for jt in range(4 * g + i + 1):
                            for er in range(2):
                                t.matmul(pb[2 * i + er][:, :],
                                         sT_buf[:, jt * GT + i * 128: jt * GT + (i + 1) * 128],
                                         v_sb[:, jt * D + half * 1024 + er * 512:
                                              jt * D + half * 1024 + (er + 1) * 512],
                                         start=(jt == 0), stop=False)
                    for d in range(ND):
                        t.wait_ge(kv_sems[d][0].h, kv_sems[d][1])
                        mm = None
                        for i in range(4):
                            for er in range(2):
                                mm = t.matmul(pb[2 * i + er][:, :],
                                              qT_g[:, d * GT + i * 128: d * GT + (i + 1) * 128],
                                              kvp_buf[:, (d % 2) * 1024 + er * 512:
                                                      (d % 2) * 1024 + (er + 1) * 512],
                                              start=False, stop=(d == ND - 1))
                        s_kvd.inc(mm)
                sec("tensor", t_o)

                for i in range(4):
                    def v_og(v, g=g, half=half, i=i, kvd_t=s_kvd.n,
                             ogt_before=s_ogt.n):
                        v.wait_ge(s_kvd.h, kvd_t)
                        if ogt_before > 0:
                            v.wait_ge(s_ogt.h, ogt_before)
                        last = None
                        for er in range(2):
                            last = v.tensor_mul(
                                og_sb[:, er * 512:(er + 1) * 512],
                                pb[2 * i + er][:, :],
                                g_sb[:, i * D + half * 1024 + er * 512:
                                     i * D + half * 1024 + (er + 1) * 512])
                        s_og.inc(last)
                    sec("vector", v_og)

                    def t_ogt(t, g=g, half=half, i=i, og_t=s_og.n):
                        t.wait_ge(s_og.h, og_t)
                        last = None
                        for et in range(8):
                            bk = pb[2 * i + et // 4][:, :].bitcast(BF16)
                            last = t.transpose(
                                bk[:, (et % 4) * 128:(et % 4 + 1) * 128],
                                og_sb[:, et * 128:(et + 1) * 128], idn[:, :])
                        s_ogt.inc(last)
                    sec("tensor", t_ogt)

                    def v_oge(v, g=g, half=half, i=i, ogt_t=s_ogt.n):
                        v.wait_ge(s_ogt.h, ogt_t)
                        last = None
                        for et in range(8):
                            eg = half * 8 + et
                            bk = pb[2 * i + et // 4][:, :].bitcast(BF16)
                            last = v.tensor_copy(
                                ogT_g[:, eg * GT + i * 128: eg * GT + (i + 1) * 128],
                                bk[:, (et % 4) * 128:(et % 4 + 1) * 128])
                        s_oge.inc(last)
                    sec("vector", v_oge)

            oge_t = s_oge.n
            for dmh in range(2):
                wp_sems = []
                for et in range(ND):
                    u_wp = (s_wp[0].n + s_wp[1].n) // 16
                    wps = s_wp[u_wp % 2]

                    def sy_wp(s, g=g, dmh=dmh, et=et, u=u_wp, wps=wps):
                        if u == 0:
                            s.wait_ge(s_cA2.h, 1)
                        if u >= 2:
                            s.wait_ge(s_wpd.h, u - 1)
                        wps.inc(s.dma_start(
                            out=wp_buf[:, (u % 2) * 1024:(u % 2 + 1) * 1024],
                            in_=aw_img(MAT["wp"], et)[:, dmh * 1024:(dmh + 1) * 1024]), 16)
                    sec("sync", sy_wp)
                    wp_sems.append((wps, wps.n))

                def t_wp(t, g=g, dmh=dmh, oge_t=oge_t, wp_sems=wp_sems,
                         aev_before=s_aev.n):
                    bank_guard(t)
                    t.wait_ge(s_oge.h, oge_t)
                    if aev_before >= 1:
                        t.wait_ge(s_aev.h, aev_before)
                    for et in range(ND):
                        t.wait_ge(wp_sems[et][0].h, wp_sems[et][1])
                        mm = None
                        for i in range(4):
                            for dr in range(2):
                                mm = t.matmul(
                                    pb[2 * i + dr][:, :],
                                    ogT_g[:, et * GT + i * 128: et * GT + (i + 1) * 128],
                                    wp_buf[:, (et % 2) * 1024 + dr * 512:
                                           (et % 2) * 1024 + (dr + 1) * 512],
                                    start=(et == 0), stop=(et == ND - 1))
                        s_wpd.inc(mm)
                sec("tensor", t_wp)

                for i in range(4):
                    def v_aev(v, g=g, dmh=dmh, i=i, wpd_t=s_wpd.n,
                              acc_before=s_acc.n):
                        v.wait_ge(s_wpd.h, wpd_t)
                        if acc_before >= 32:
                            v.wait_ge(s_acc.h, acc_before - 16)
                        last = None
                        for dr in range(2):
                            last = v.tensor_copy(
                                att_sb[:, (i % 2) * 1024 + dr * 512:
                                       (i % 2) * 1024 + (dr + 1) * 512],
                                pb[2 * i + dr][:, :])
                        s_aev.inc(last)
                    sec("vector", v_aev)

                    def g_acc(gp, g=g, dmh=dmh, i=i, aev_t=s_aev.n, x2i_t=s_x2i.n):
                        gp.wait_ge(s_aev.h, aev_t)
                        gp.wait_ge(s_x2i.h, x2i_t)
                        s_acc.inc(gp.dma_start(
                            out=x2_dram[(g * 4 + i) * 128:(g * 4 + i + 1) * 128,
                                        dmh * 1024:(dmh + 1) * 1024],
                            in_=att_sb[:, (i % 2) * 1024:(i % 2 + 1) * 1024],
                            accum_op=ALU.add), 16)
                        gp.wait_ge(s_acc.h, s_acc.n)
                    sec("gpsimd", g_acc)

        x2g_f = [ktj_sb[:, :].bitcast(F32), ogT_g[:, :].bitcast(F32)]

        def x2tile(i):
            return x2g_f[i // 2][:, (i % 2) * D:(i % 2 + 1) * D]

        xn2T = xnT_g
        hidT = v_sb
        t1v = kvp_buf[:, :].bitcast(F32)
        wd_buf2 = wp_buf
        out_sb = att_sb
        s_yo2 = Sem("s_yo2")
        acc_total = s_acc.n

        for g3 in range(NG):
            g4 = g3 * 4
            for i in range(4):
                ti = g4 + i
                n_tile = tile_ctr[0]
                tile_ctr[0] += 1
                sxs = s_x2l[i]

                def sy_x2(s, ti=ti, i=i, g3=g3, sxs=sxs, fev_t=8 * g3):
                    s.wait_ge(s_acc.h, acc_total)
                    if fev_t > 0:
                        s.wait_ge(s_fev.h, fev_t)
                    sxs.inc(s.dma_start(out=x2tile(i),
                                        in_=x2_dram[ti * 128:(ti + 1) * 128, :]), 16)
                sec("sync", sy_x2)
                x2_target = sxs.n

                base = ti * 8
                col = lambda k, base=base: stats[:, base + k:base + k + 1]
                w1_c, sq1_c, r1_c = col(0), col(1), col(2)
                m_c = maggr[:, 2 * ti:2 * ti + 1]
                v_c = maggr[:, 2 * ti + 1:2 * ti + 2]

                def v_stats2(v, ti=ti, i=i, x2_target=x2_target, sxs=sxs,
                             w1_c=w1_c, sq1_c=sq1_c, r1_c=r1_c, m_c=m_c, v_c=v_c,
                             n_tile=n_tile):
                    v.wait_ge(sxs.h, x2_target)
                    for cq in range(4):
                        v.bn_stats(bnbuf[:, cq * 6:(cq + 1) * 6],
                                   x2tile(i)[:, cq * 512:(cq + 1) * 512])
                    v.drain()
                    v.bn_aggr(maggr[:, 2 * ti:2 * ti + 2], bnbuf[:, 0:24])
                    v.drain()
                    sv.inc(v.tensor_scalar_add(w1_c, v_c, EPS_LN))
                    v.wait_ge(sc_.h, sc_.n + 1)
                    v.reciprocal(r1_c, sq1_c)
                    v.drain()
                    v.tensor_copy(cs_all[:, ti:ti + 1], r1_c)
                    v.drain()
                    if n_tile >= 2:
                        v.wait_ge(s_tpe.h, tile_marks[n_tile - 2])
                    s_xn.inc(v.tensor_scalar(
                        xn_sb[:, (n_tile % 2) * D:(n_tile % 2 + 1) * D],
                        x2tile(i),
                        m_c, cs_all[:, ti:ti + 1], ALU.subtract, ALU.mult))
                sec("vector", v_stats2)

                def sc_chain2(s, w1_c=w1_c, sq1_c=sq1_c, sv_base=sv.n):
                    s.wait_ge(sv.h, sv_base)
                    sc_.inc(s.activation(sq1_c, w1_c, AF.Sqrt))
                sec("scalar", sc_chain2)

                xn_target = s_xn.n
                pgs_x = []

                def t_xtr2(t, n_tile=n_tile, pgs_x=pgs_x, xn_target=xn_target,
                           first=(i == 0)):
                    def w0(tt):
                        if first:
                            bank_guard(tt)
                        tt.wait_ge(s_xn.h, xn_target)
                    for pr in range(8):
                        pgs_x.append(emit_transpose_unit(
                            t,
                            [xn_sb[:, (n_tile % 2) * D + (2 * pr) * 128:
                                   (n_tile % 2) * D + (2 * pr + 1) * 128],
                             xn_sb[:, (n_tile % 2) * D + (2 * pr + 1) * 128:
                                   (n_tile % 2) * D + (2 * pr + 2) * 128]],
                            extra_wait=w0 if pr == 0 else None))
                sec("tensor", t_xtr2)

                def v_xev2(v, i=i, pgs_x=pgs_x):
                    for pr in range(8):
                        d0, d1 = 2 * pr, 2 * pr + 1
                        emit_evac_unit(v, pgs_x[pr], [
                            xn2T[:, d0 * GT + i * 128: d0 * GT + (i + 1) * 128],
                            xn2T[:, d1 * GT + i * 128: d1 * GT + (i + 1) * 128]])
                sec("vector", v_xev2)
                tile_marks.append(s_tpe.n)

            tpe_now = s_tp.n
            for b in range(64):
                ub = g3 * 64 + b
                slot = ub % 2
                fus = [s_fu, s_fu2][slot]
                fgs = [s_fg, s_fg2][slot]

                def g_ld_ug(gp, b=b, ub=ub, slot=slot, fus=fus, fgs=fgs):
                    if ub == 0:
                        gp.wait_ge(s_cFN.h, 1)
                    if ub >= 2:
                        gp.wait_ge(s_hid.h, ub - 1)
                    fus.inc(gp.dma_start(out=wo_buf[:, slot * D:(slot + 1) * D],
                                         in_=ffn_img(0, b)), 16)
                    fgs.inc(gp.dma_start(out=ktjs_buf[:, slot * D:(slot + 1) * D],
                                         in_=ffn_img(1, b)), 16)
                sec("gpsimd", g_ld_ug)
                fu_t, fg_t = fus.n, fgs.n

                def t_ug(t, b=b, ub=ub, slot=slot, fus=fus, fgs=fgs,
                         fu_t=fu_t, fg_t=fg_t, g3=g3, tpe_now=tpe_now,
                         first=(b == 0)):
                    if first:
                        bank_guard(t)
                        t.wait_ge(s_tpe.h, tpe_now)
                    t.wait_ge(fus.h, fu_t)
                    t.wait_ge(fgs.h, fg_t)
                    if ub >= 2:
                        t.wait_ge(s_hid.h, ub - 1)
                    mm = None
                    for d in range(ND):
                        t.matmul(pb[2 * slot][:, :],
                                 wo_buf[:, slot * D + d * 128: slot * D + (d + 1) * 128],
                                 xn2T[:, d * GT:(d + 1) * GT],
                                 start=(d == 0), stop=(d == ND - 1))
                    for d in range(ND):
                        mm = t.matmul(pb[2 * slot + 1][:, :],
                                      ktjs_buf[:, slot * D + d * 128: slot * D + (d + 1) * 128],
                                      xn2T[:, d * GT:(d + 1) * GT],
                                      start=(d == 0), stop=(d == ND - 1))
                    s_fud.inc(mm)
                sec("tensor", t_ug)

                def s_t1f(s, ub=ub, slot=slot, fud_t=s_fud.n, scb_t=1):
                    s.wait_ge(s_fud.h, fud_t)
                    if ub == 0:
                        s.wait_ge(s_scb.h, scb_t)
                    s_t1.inc(s.activation(t1v[:, slot * 512:(slot + 1) * 512],
                                          pb[2 * slot][:, :], AF.Silu,
                                          scale=scbc[:, 14:15]))
                sec("scalar", s_t1f)

                def v_hid(v, b=b, ub=ub, slot=slot, g3=g3, t1_t=s_t1.n):
                    v.wait_ge(s_t1.h, t1_t)
                    s_hid.inc(v.scalar_tensor_tensor(
                        hidT[:, b * 512:(b + 1) * 512],
                        pb[2 * slot + 1][:, :], scbc[:, 15:16],
                        t1v[:, slot * 512:(slot + 1) * 512],
                        ALU.mult, ALU.mult))
                sec("vector", v_hid)

            for dmh in range(2):
                fd_sems = []
                for e8 in range(64):
                    ud = (s_fd.n + s_fd2.n) // 16
                    fds = [s_fd, s_fd2][ud % 2]

                    def g_ld_d(gp, e8=e8, dmh=dmh, ud=ud, fds=fds):
                        if ud >= 2:
                            gp.wait_ge(s_dd.h, ud - 1)
                        fds.inc(gp.dma_start(
                            out=wd_buf2[:, (ud % 2) * 1024:(ud % 2 + 1) * 1024],
                            in_=ffn_img(2, e8)[:, dmh * 1024:(dmh + 1) * 1024]), 16)
                    sec("gpsimd", g_ld_d)
                    fd_sems.append((fds, fds.n, ud % 2))

                def t_dn(t, g3=g3, dmh=dmh, fd_sems=fd_sems, hid_t=s_hid.n):
                    bank_guard(t)
                    t.wait_ge(s_hid.h, hid_t)
                    for e8 in range(64):
                        fds, tgt, slot = fd_sems[e8]
                        t.wait_ge(fds.h, tgt)
                        mm = None
                        for i in range(4):
                            for dr in range(2):
                                mm = t.matmul(
                                    pb[2 * i + dr][:, :],
                                    hidT[:, e8 * 512 + i * 128: e8 * 512 + (i + 1) * 128],
                                    wd_buf2[:, slot * 1024 + dr * 512:
                                            slot * 1024 + (dr + 1) * 512],
                                    start=(e8 == 0), stop=(e8 == 63))
                        s_dd.inc(mm)
                sec("tensor", t_dn)

                for i in range(4):
                    unit = (g3 * 2 + dmh) * 4 + i
                    yos = [s_yo, s_yo2][unit % 2]

                    def v_fevf(v, g3=g3, dmh=dmh, i=i, unit=unit, dd_t=s_dd.n,
                               yos=yos, yo_t=yos.n):
                        v.wait_ge(s_dd.h, dd_t)
                        if yo_t > 0:
                            v.wait_ge(yos.h, yo_t)
                        v.tensor_sub(out_sb[:, (unit % 2) * 1024:(unit % 2) * 1024 + 512],
                                     x2tile(i)[:, dmh * 1024: dmh * 1024 + 512],
                                     pb[2 * i][:, :])
                        s_fev.inc(v.tensor_sub(
                            out_sb[:, (unit % 2) * 1024 + 512:(unit % 2 + 1) * 1024],
                            x2tile(i)[:, dmh * 1024 + 512:(dmh + 1) * 1024],
                            pb[2 * i + 1][:, :]))
                    sec("vector", v_fevf)

                    def sy_y(s, g3=g3, dmh=dmh, i=i, unit=unit, yos=yos,
                             fev_t=s_fev.n):
                        s.wait_ge(s_fev.h, fev_t)
                        yos.inc(s.dma_start(
                            out=y_sh[(g3 * 4 + i) * 128:(g3 * 4 + i + 1) * 128,
                                     dmh * 1024:(dmh + 1) * 1024],
                            in_=out_sb[:, (unit % 2) * 1024:(unit % 2 + 1) * 1024]), 16)
                    sec("sync", sy_y)

        def sy_fin(s, a=s_yo.n, b=s_yo2.n):
            s.wait_ge(s_yo.h, a)
            s.wait_ge(s_yo2.h, b)
        sec("sync", sy_fin)

    return nc




_CACHE = {}


def kernel(L0_attn_x, L0_attn_Wq, L0_attn_Wk, L0_attn_Wv, L0_attn_Wg, L0_attn_Wp,
           L0_ffn_W_up, L0_ffn_W_gate, L0_ffn_W_down):
    if "nc" not in _CACHE:
        _CACHE["nc"] = build()
    nc = _CACHE["nc"]
    X = np.ascontiguousarray(np.asarray(L0_attn_x, dtype=np.float32).reshape(NCORES * TOK, D))
    Wq = np.asarray(L0_attn_Wq, dtype=np.float32)
    Wk = np.asarray(L0_attn_Wk, dtype=np.float32)
    Wv = np.asarray(L0_attn_Wv, dtype=np.float32)
    Wg = np.asarray(L0_attn_Wg, dtype=np.float32)
    Wp = np.asarray(L0_attn_Wp, dtype=np.float32)
    Wu = np.asarray(L0_ffn_W_up, dtype=np.float32)
    Wg2 = np.asarray(L0_ffn_W_gate, dtype=np.float32)
    Wd = np.asarray(L0_ffn_W_down, dtype=np.float32)
    in_maps = []
    for c in range(NCORES):
        in_maps.append({
            "x_sh": np.ascontiguousarray(X[c * TOK:(c + 1) * TOK]),
            "wk_sh": np.ascontiguousarray(Wk[c * 256:(c + 1) * 256, :]),
            "wq_sh": np.ascontiguousarray(Wq[c * 256:(c + 1) * 256, :]),
            "wv_sh": np.ascontiguousarray(Wv[:, c * 256:(c + 1) * 256]),
            "wg_sh": np.ascontiguousarray(Wg[:, c * 256:(c + 1) * 256]),
            "wp_sh": np.ascontiguousarray(Wp[:, c * 256:(c + 1) * 256]),
            "kvmask": (np.arange(8) < c).astype(np.int32).reshape(1, 8),
            "wu_sh": np.ascontiguousarray(Wu[c * 1024:(c + 1) * 1024, :]),
            "wg2_sh": np.ascontiguousarray(Wg2[c * 1024:(c + 1) * 1024, :]),
            "wd_sh": np.ascontiguousarray(Wd[:, c * 1024:(c + 1) * 1024]),
        })
    res = run_bass_kernel_spmd(nc, in_maps, core_ids=list(range(NCORES)))
    out = np.concatenate([res.results[c]["y_sh"] for c in range(NCORES)], axis=0)
    return out.reshape(1, NCORES * TOK, D)
